# version 04
# baseline (speedup 1.0000x reference)
import numpy as np
import ml_dtypes
from contextlib import ExitStack

import concourse.bass as bass
import concourse.mybir as mybir
from concourse.bass_utils import run_bass_kernel_spmd

F32 = mybir.dt.float32
BF16 = mybir.dt.bfloat16
F16 = mybir.dt.float16
I32 = mybir.dt.int32
AF = mybir.ActivationFunctionType
ALU = mybir.AluOpType
AX = mybir.AxisListType

NC = 8
S, H, SC = 8192, 2048, 1024
NH, D, HL = 16, 128, 2
IFF, OFF = 5632, 11264
GU_SH, DN_SH = OFF // NC, H // NC
RSCL = float(1.0 / np.sqrt(128.0))
MARG = 1.0816
LN_EPS, RMS_EPS = 1e-5, 1e-6

_DEBUG = set()

SEM_NAMES = (
    "sIN", "sGIN", "sLNW", "sCC", "sRHS", "sRHS2", "sWV", "sWO",
    "sT1", "sVE", "sTV", "sEVV", "sTQ", "sEVQ", "sTM", "sVM", "sSM",
    "sKW", "sKB",
    "sSC0", "sSC1", "sEX0", "sEX1", "sR0", "sR1", "sCM0", "sCM1",
    "sDQ", "sDQT", "sDQE", "sDQW", "sDQW2",
    "sCW", "sCA", "sTA", "sVA", "sSA", "sXR",
    "sGWg", "sGWu", "sDW", "sYEV", "sX2S", "sX2R", "sOUT", "sDBG",
)

DN_CHUNKS = [(0, 0, 256), (0, 256, 256), (0, 512, 192),
             (1, 0, 256), (1, 256, 256), (1, 512, 192)]
NUNITS = 11 + len(DN_CHUNKS)


def _newton_rstd(v, st, vcol, sqcol, r0col, t4col, outcol):
    v.reciprocal(st[:, r0col, :], st[:, sqcol, :])
    v.tensor_tensor(st[:, t4col, :], st[:, r0col, :], st[:, r0col, :], ALU.mult)
    v.tensor_tensor(st[:, t4col, :], st[:, t4col, :], st[:, vcol, :], ALU.mult)
    v.tensor_scalar(st[:, t4col, :], st[:, t4col, :], -0.5, 1.5, ALU.mult, ALU.add)
    v.tensor_tensor(st[:, outcol, :], st[:, r0col, :], st[:, t4col, :], ALU.mult)


class _VDrain:

    _OPS = {"tensor_scalar", "tensor_tensor", "tensor_copy", "tensor_add",
            "tensor_sub", "tensor_mul", "tensor_max", "reduce_sum",
            "reduce_max", "tensor_reduce", "bn_stats", "bn_aggr", "select",
            "reciprocal", "memset", "scalar_tensor_tensor", "tensor_relu",
            "tensor_scalar_add", "tensor_scalar_sub", "tensor_scalar_mul"}

    def __init__(self, eng):
        self._e = eng

    def __getattr__(self, n):
        f = getattr(self._e, n)
        if n in self._OPS:
            def g(*a, **k):
                r = f(*a, **k)
                self._e.drain()
                return r
            return g
        return f


def build_nc(debug=()):
    debug = set(debug)
    nc = bass.Bass("TRN2", target_bir_lowering=False,
                  dynamic_dma_scratch_size=16384)

    x_d = nc.dram_tensor("x", [SC, H], F32, kind="ExternalInput")
    wqk_d = nc.dram_tensor("wqk", [H, 4 * D], F32, kind="ExternalInput")
    wv_d = nc.dram_tensor("wv", [H, H], F32, kind="ExternalInput")
    wo_d = nc.dram_tensor("wo", [H, H], F32, kind="ExternalInput")
    gup_d = nc.dram_tensor("gup", [GU_SH, 256], I32, kind="ExternalInput")
    gsc_d = nc.dram_tensor("gsc", [GU_SH, 16], F16, kind="ExternalInput")
    dnp_d = nc.dram_tensor("dnp", [DN_SH, 704], I32, kind="ExternalInput")
    dsc_d = nc.dram_tensor("dsc", [DN_SH, 44], F16, kind="ExternalInput")
    eye_d = nc.dram_tensor("eye", [128, 128], BF16, kind="ExternalInput")
    meta_d = nc.dram_tensor("meta", [1, 2], I32, kind="ExternalInput")
    y_d = nc.dram_tensor("y", [SC, H], F32, kind="ExternalOutput")

    ln_in = nc.dram_tensor("cc_ln_in", [H, SC], BF16)
    ln_out = nc.dram_tensor("cc_ln_out", [NC, H, SC], BF16)
    c_in = nc.dram_tensor("cc_c_in", [HL, 64, 128], BF16)
    c_out = nc.dram_tensor("cc_c_out", [NC, HL, 64, 128], BF16)
    gu_in = nc.dram_tensor("cc_gu_in", [128, 11, 2048], BF16)
    gu_out = nc.dram_tensor("cc_gu_out", [NC, 128, 11, 2048], BF16)
    dn_in = nc.dram_tensor("cc_dn_in", [44, 128, 256], BF16)
    dn_out = nc.dram_tensor("cc_dn_out", [NC, 44, 128, 256], BF16)
    x2_h = nc.dram_tensor("x2_h", [SC, H], F32)

    dbg = {}

    def dbg_out(name, shape, dt):
        if name in debug:
            dbg[name] = nc.dram_tensor("d_" + name, shape, dt,
                                       kind="ExternalOutput")
    dbg_out("lnag", [NC, H, SC], BF16)
    dbg_out("qk", [128, 4, S], BF16)
    dbg_out("v", [128, 8, H], BF16)
    dbg_out("negm", [128, HL, 64], F32)
    dbg_out("cga", [NC, HL, 64, 128], BF16)
    dbg_out("x2", [SC, H], F32)
    dbg_out("wgu", [NC, 128, 11, 2048], BF16)
    dbg_out("wdn", [NC, 44, 128, 256], BF16)
    dbg_out("hact", [128, 44, SC], BF16)
    dbg_out("scl0", [128, 11, 16], F32)
    dbg_out("scl1", [128, 11, 16], F32)
    dbg_out("dsc1", [128, 2, 44], F32)
    dbg_out("kmx", [128, HL], F32)
    dbg_out("kks", [1, 2], F32)
    dbg_out("kk8d", [1, 8], BF16)
    dbg_out("kms", [128, 2], F32)
    dbg_out("psc", [128, 2, 64], F32)
    dbg_out("pst", [1, 2, 128], F32)
    dbg_out("q2s", [128, 64], BF16)
    dbg_out("on8", [128, 8], BF16)
    dbg_out("eyeE", [128, 128], BF16)
    dbg_out("eyeL", [128, 128], BF16)
    for _n in ("ey0", "ey1", "ey2", "ey3", "ey4"):
        dbg_out(_n, [128, 128], BF16)
    dbg_out("gupd", [128, 11, 256], I32)
    dbg_out("q2t", [128, 128], BF16)

    n_dbg = [0]

    with ExitStack() as top:
        E = top.enter_context
        blk = E(nc.Block())
        sems = {n: E(nc.semaphore(n)) for n in SEM_NAMES}

        def sm(n):
            return sems[n]

        def dump(name, src_ap, wait_sem, wait_val):
            if name not in dbg:
                return
            @blk.gpsimd
            def _(g):
                g.wait_ge(sm(wait_sem), wait_val)
                g.dma_start(out=dbg[name].ap(), in_=src_ap
                            ).then_inc(sm("sDBG"), 16)
            n_dbg[0] += 16

        eye_s = E(nc.sbuf_tensor("eye_s", [128, 128], BF16))
        ones8 = E(nc.sbuf_tensor("ones8", [128, 8], BF16))
        meta_s = E(nc.sbuf_tensor("meta_s", [1, 2], I32))
        gup_s = E(nc.sbuf_tensor("gup_s", [128, 11, 256], I32))
        gscf = E(nc.sbuf_tensor("gscf", [128, 11, 16], F32))
        dnp_s = E(nc.sbuf_tensor("dnp_s", [128, 2, 704], I32))
        dscf = E(nc.sbuf_tensor("dscf", [128, 2, 44], F32))
        ln1t = E(nc.sbuf_tensor("lnt", [128, 16, SC], BF16))
        v_s = E(nc.sbuf_tensor("v_s", [128, 8, H], BF16))

        ph1 = ExitStack()
        if True:
            P1 = ph1.enter_context
            gsc_s = P1(nc.sbuf_tensor("gsc_s", [128, 11, 16], F16))
            dsc_s = P1(nc.sbuf_tensor("dsc_s", [128, 2, 44], F16))
            x_s = P1(nc.sbuf_tensor("x_s", [128, 8, H], F32))
            lnbuf = P1(nc.sbuf_tensor("lnbuf", [128, 8, H], BF16))
            bns = P1(nc.sbuf_tensor("bns", [128, 8, 4, 6], F32))
            mv = P1(nc.sbuf_tensor("mv", [128, 8, 2], F32))
            st1 = P1(nc.sbuf_tensor("st1", [128, 6, 8], F32))
            pst = P1(nc.psum_tensor("pst", [128, 2, 1024], BF16))

            @blk.sync
            def _(sy):
                sy.dma_start(out=eye_s[:], in_=eye_d.ap()).then_inc(sm("sIN"), 16)
                sy.dma_start(out=meta_s[:], in_=meta_d.ap()).then_inc(sm("sIN"), 16)
                for t in range(8):
                    sy.dma_start(out=x_s[:, t, :],
                                 in_=x_d.ap()[128 * t:128 * (t + 1), :]
                                 ).then_inc(sm("sIN"), 16)
            M_in = 16 * 10

            @blk.gpsimd
            def _(g):
                for t in range(11):
                    g.dma_start(out=gup_s[:, t, :],
                                in_=gup_d.ap()[128 * t:128 * (t + 1), :]
                                ).then_inc(sm("sGIN"), 16)
                for t in range(11):
                    g.dma_start(out=gsc_s[:, t, :],
                                in_=gsc_d.ap()[128 * t:128 * (t + 1), :]
                                ).then_inc(sm("sGIN"), 16)
                for t in range(2):
                    g.dma_start(out=dnp_s[:, t, :],
                                in_=dnp_d.ap()[128 * t:128 * (t + 1), :]
                                ).then_inc(sm("sGIN"), 16)
                for t in range(2):
                    g.dma_start(out=dsc_s[:, t, :],
                                in_=dsc_d.ap()[128 * t:128 * (t + 1), :]
                                ).then_inc(sm("sGIN"), 16)
            M_gin_packed = 16 * 26

            @blk.vector
            def _(v):
                v = _VDrain(v)
                v.memset(ones8[:], 1.0)
                v.wait_ge(sm("sGIN"), M_gin_packed)
                v.tensor_copy(gscf[:], gsc_s[:])
                v.tensor_copy(dscf[:], dsc_s[:])
                v.wait_ge(sm("sIN"), M_in)
                for t in range(8):
                    for k in range(4):
                        v.bn_stats(bns[:, t, k, :], x_s[:, t, 512 * k:512 * (k + 1)])
                    v.bn_aggr(mv[:, t, :], bns[:, t, :, :])
                v.tensor_scalar(st1[:, 0, :], mv[:, :, 0], -1.0, None, ALU.mult)
                v.tensor_scalar(st1[:, 1, :], mv[:, :, 1], LN_EPS, None, ALU.add)
                v.engine_nop().then_inc(sm("sVE"), 1)
            M_var1 = 1

            @blk.scalar
            def _(sc):
                sc.wait_ge(sm("sVE"), M_var1)
                sc.activation(st1[:, 2, :], st1[:, 1, :], AF.Sqrt
                              ).then_inc(sm("sSM"), 1)
            M_sq1 = 1

            @blk.vector
            def _(v):
                v = _VDrain(v)
                v.wait_ge(sm("sSM"), M_sq1)
                _newton_rstd(v, st1, 1, 2, 3, 4, 5)
                for t in range(8):
                    v.tensor_scalar(lnbuf[:, t, :], x_s[:, t, :],
                                    st1[:, 0, t:t + 1], st1[:, 5, t:t + 1],
                                    ALU.add, ALU.mult)
                v.engine_nop().then_inc(sm("sVE"), 1)
            M_ln1 = M_var1 + 1

            dump("scl0", gscf[:], "sVE", M_var1)

            @blk.tensor
            def _(t_):
                t_.wait_ge(sm("sVE"), M_ln1)
                t_.wait_ge(sm("sIN"), M_in)
                for i in range(128):
                    st_, ft = divmod(i, 16)
                    if i >= 2:
                        t_.wait_ge(sm("sVE"), M_ln1 + i - 1)
                    t_.transpose(pst[:, i % 2, 0:128],
                                 lnbuf[:, st_, 128 * ft:128 * (ft + 1)],
                                 eye_s[:]).then_inc(sm("sT1"), 1)

            @blk.vector
            def _(v):
                v = _VDrain(v)
                for i in range(128):
                    st_, ft = divmod(i, 16)
                    v.wait_ge(sm("sT1"), i + 1)
                    v.tensor_copy(ln1t[:, ft, 128 * st_:128 * (st_ + 1)],
                                  pst[:, i % 2, 0:128]).then_inc(sm("sVE"), 1)
            M_ln1t = M_ln1 + 128

            @blk.scalar
            def _(sc):
                sc.wait_ge(sm("sVE"), M_ln1t)
                for ft in range(16):
                    sc.dma_start(
                        out=bass.AP(ln_in, 128 * ft * SC, [[SC, 128], [1, SC]]),
                        in_=ln1t[:, ft, :]).then_inc(sm("sLNW"), 16)
            M_lnw = 16 * 16

            dump("ey0", eye_s[:], "sT1", 128)

        ph1.close()
        stk_qk = ExitStack()
        qk_s = stk_qk.enter_context(nc.sbuf_tensor("qk_s", [128, 4, S], BF16))

        with ExitStack() as phB:
            PB = phB.enter_context
            wqk_s = PB(nc.sbuf_tensor("wqk_s", [128, 16, 4 * D], BF16))
            rhs_b = PB(nc.sbuf_tensor("rhs_b", [128, 2, 16, 512], BF16))
            wv_b = rhs_b
            psv = PB(nc.psum_tensor("psv", [128, 2, 512], F32))
            psq = PB(nc.psum_tensor("psq", [128, 4, 512], F32))

            @blk.gpsimd
            def _(g):
                g.wait_ge(sm("sT1"), 128)
                for oc in range(4):
                    if oc >= 1:
                        g.wait_ge(sm("sTV"), oc * 8)
                        g.wait_ge(sm("sWV"), 256 * oc)
                    for t in range(16):
                        g.dma_start(out=wv_b[:, 0, t, :],
                                    in_=wv_d.ap()[128 * t:128 * (t + 1),
                                                  512 * oc:512 * (oc + 1)]
                                    ).then_inc(sm("sWV"), 16)
                if "ey1" in dbg:
                    g.dma_start(out=dbg["ey1"].ap(), in_=eye_s[:]
                                ).then_inc(sm("sDBG"), 16)
                g.wait_ge(sm("sGIN"), M_gin_packed)
                for t in range(16):
                    g.dma_start(out=wqk_s[:, t, :],
                                in_=wqk_d.ap()[128 * t:128 * (t + 1), :]
                                ).then_inc(sm("sGIN"), 16)
                if "ey2" in dbg:
                    g.dma_start(out=dbg["ey2"].ap(), in_=eye_s[:]
                                ).then_inc(sm("sDBG"), 16)
                g.wait_ge(sm("sLNW"), M_lnw)
                g.collective_compute(
                    "AllGather", ALU.bypass, replica_groups=[list(range(NC))],
                    ins=[ln_in.ap().opt()], outs=[ln_out.ap().opt()],
                ).then_inc(sm("sCC"), 1)
                if "ey3" in dbg:
                    g.wait_ge(sm("sCC"), 1)
                    g.dma_start(out=dbg["ey3"].ap(), in_=eye_s[:]
                                ).then_inc(sm("sDBG"), 16)
            M_cc_ln = 1
            M_gin = M_gin_packed + 16 * 16
            for _n in ("ey1", "ey2", "ey3"):
                if _n in dbg:
                    n_dbg[0] += 16

            @blk.tensor
            def _(t_):
                t_.wait_ge(sm("sVE"), M_ln1t)
                for oc in range(4):
                    t_.wait_ge(sm("sWV"), 256 * (oc + 1))
                    for st_ in range(8):
                        gidx = oc * 8 + st_
                        if gidx >= 2:
                            t_.wait_ge(sm("sEVV"), gidx - 1)
                        for ht in range(16):
                            mm = t_.matmul(psv[:, gidx % 2, :],
                                           ln1t[:, ht, 128 * st_:128 * (st_ + 1)],
                                           wv_b[:, 0, ht, :],
                                           start=(ht == 0), stop=(ht == 15))
                        mm.then_inc(sm("sTV"), 1)

            @blk.vector
            def _(v):
                v = _VDrain(v)
                for gidx in range(32):
                    oc, st_ = divmod(gidx, 8)
                    v.wait_ge(sm("sTV"), gidx + 1)
                    v.tensor_copy(v_s[:, st_, 512 * oc:512 * (oc + 1)],
                                  psv[:, gidx % 2, :]).then_inc(sm("sEVV"), 1)
            M_evv = 32

            @blk.sync
            def _(sy):
                sy.wait_ge(sm("sCC"), M_cc_ln)
                sy.wait_ge(sm("sTV"), 32)
                for i in range(16):
                    r, scn = divmod(i, 2)
                    if i >= 1:
                        sy.wait_ge(sm("sTQ"), max(0, (i - 1)) * 4)
                        sy.wait_ge(sm("sRHS"), 128 * i)
                    for ht in range(8):
                        sy.dma_start(
                            out=rhs_b[:, i % 2, ht, :],
                            in_=ln_out.ap()[r][128 * ht:128 * (ht + 1),
                                               512 * scn:512 * (scn + 1)]
                        ).then_inc(sm("sRHS"), 16)

            @blk.scalar
            def _(sc):
                sc.wait_ge(sm("sCC"), M_cc_ln)
                sc.wait_ge(sm("sTV"), 32)
                for i in range(16):
                    r, scn = divmod(i, 2)
                    if i >= 1:
                        sc.wait_ge(sm("sTQ"), max(0, (i - 1)) * 4)
                        sc.wait_ge(sm("sRHS2"), 128 * i)
                    for ht in range(8, 16):
                        sc.dma_start(
                            out=rhs_b[:, i % 2, ht, :],
                            in_=ln_out.ap()[r][128 * ht:128 * (ht + 1),
                                               512 * scn:512 * (scn + 1)]
                        ).then_inc(sm("sRHS2"), 16)

            @blk.tensor
            def _(t_):
                t_.wait_ge(sm("sGIN"), M_gin)
                for i in range(16):
                    t_.wait_ge(sm("sRHS"), 128 * (i + 1))
                    t_.wait_ge(sm("sRHS2"), 128 * (i + 1))
                    if i >= 1:
                        t_.wait_ge(sm("sEVQ"), 4 * i)
                    for ft in range(4):
                        for ht in range(16):
                            mm = t_.matmul(psq[:, ft, :],
                                           wqk_s[:, ht, 128 * ft:128 * (ft + 1)],
                                           rhs_b[:, i % 2, ht, :],
                                           start=(ht == 0), stop=(ht == 15))
                        mm.then_inc(sm("sTQ"), 1)

            @blk.vector
            def _(v):
                v = _VDrain(v)
                for i in range(16):
                    r, scn = divmod(i, 2)
                    off = 1024 * r + 512 * scn
                    for ft in range(4):
                        v.wait_ge(sm("sTQ"), 4 * i + ft + 1)
                        v.tensor_copy(qk_s[:, ft, off:off + 512],
                                      psq[:, ft, :]).then_inc(sm("sEVQ"), 1)
            M_evq = 64

            dump("qk", qk_s[:], "sEVQ", M_evq)
            dump("v", v_s[:], "sEVV", M_evv)
            dump("lnag", ln_out.ap(), "sCC", M_cc_ln)
            dump("ey4", eye_s[:], "sEVQ", M_evq)

        with ExitStack() as p5:
            P = p5.enter_context
            e_s = P(nc.sbuf_tensor("e_s", [128, HL, S], BF16))
            q2_s = e_s[:, 0, :]
            negm = P(nc.sbuf_tensor("negm", [128, HL, 64], F32))
            zf = P(nc.sbuf_tensor("zf", [128, HL, 2, 16], F32))
            rb8 = P(nc.sbuf_tensor("rb8", [128, HL, 2, 8], BF16))
            zs = P(nc.sbuf_tensor("zs", [128, HL, 4], F32))
            kmxf = P(nc.sbuf_tensor("kmxf", [128, HL], F32))
            kk_sb = P(nc.sbuf_tensor("kk_sb", [1, 2], F32))
            ones_r = P(nc.sbuf_tensor("ones_r", [1, 128], BF16))
            if "kms" in dbg:
                kms_dbg = P(nc.sbuf_tensor("kms_dbg", [128, 2], F32))
                psc_dbg = P(nc.sbuf_tensor("psc_dbg", [128, 2, 64], F32))
                q2s_dbg = P(nc.sbuf_tensor("q2s_dbg", [128, 64], BF16))
            if "q2t" in dbg:
                q2t_dbg = P(nc.sbuf_tensor("q2t_dbg", [128, 128], BF16))
            if "pst" in dbg:
                pst_dbg = P(nc.sbuf_tensor("pst_dbg", [1, 2, 128], F32))
            kk8 = P(nc.sbuf_tensor("kk8", [1, 8], BF16))
            mh_sc = P(nc.sbuf_tensor("mh_sc", [128, 64], F32))
            c_sb = P(nc.sbuf_tensor("c_sb", [128, HL, 64], F32))
            cbf = P(nc.sbuf_tensor("cbf", [128, HL, 64], BF16))
            c_tb = P(nc.sbuf_tensor("c_tb", [64, HL, 128], BF16))
            dq_mag = P(nc.sbuf_tensor("dq_mag", [128, 1024], I32))
            dq_sg = P(nc.sbuf_tensor("dq_sg", [128, 1024], I32))
            dq_a = P(nc.sbuf_tensor("dq_a", [128, 1024], BF16))
            dq_b = P(nc.sbuf_tensor("dq_b", [128, 1024], BF16))
            dq_cw = P(nc.sbuf_tensor("dq_cw", [128, 1024], BF16))
            dq_w = P(nc.sbuf_tensor("dq_w", [128, 2, 2048], BF16))
            dq_stg = P(nc.sbuf_tensor("dq_stg", [128, 1, 2048], BF16))
            dq_dnb = P(nc.sbuf_tensor("dq_dnb", [128, 4, 128], BF16))
            ps_s = P(nc.psum_tensor("ps_s", [128, HL, 2, 512], F32))
            ps_c = P(nc.psum_tensor("ps_c", [128, HL, 512], F32))
            ps_t = P(nc.psum_tensor("ps_t", [128, 2, 1024], BF16))

            vm = 0
            tm = 0
            smm = M_sq1
            for h in range(HL):
                vm0, tm0 = vm, tm

                @blk.vector
                def _(v, h=h, vm0=vm0):
                    v = _VDrain(v)
                    if h == 0:
                        v.wait_ge(sm("sEVQ"), M_evq)
                        v.memset(rb8[:], 0.0)
                        v.memset(ones_r[:], 1.0)
                    v.tensor_tensor(q2_s[:], qk_s[:, 2 + h, :],
                                    qk_s[:, 2 + h, :], ALU.mult)
                    v.engine_nop().then_inc(sm("sVM"), 1)
                vm += 1

                @blk.tensor
                def _(t_, vm_=vm, h=h):
                    t_.wait_ge(sm("sVM"), vm_)
                    if h == 0:
                        t_.wait_ge(sm("sEVV"), M_evv)
                    for kt in range(64):
                        mm = t_.matmul(ps_c[:, 0, 8 * kt:8 * kt + 8],
                                       q2_s[:, 128 * kt:128 * (kt + 1)],
                                       ones8[:], start=(kt == 0), stop=True,
                                       skip_group_check=True)
                    mm.then_inc(sm("sTM"), 1)
                tm += 1

                @blk.vector
                def _(v, tm_=tm, h=h):
                    v = _VDrain(v)
                    v.wait_ge(sm("sTM"), tm_)
                    v.reduce_max(kmxf[:, h:h + 1], ps_c[:, 0, 0:512:8], axis=AX.X)
                    if "kms" in dbg:
                        v.tensor_copy(kms_dbg[:, h:h + 1], kmxf[:, h:h + 1])
                        v.tensor_copy(psc_dbg[:, h, :], ps_c[:, 0, 0:512:8])
                        if h == 0:
                            v.tensor_copy(q2s_dbg[:], q2_s[:, 0:8192:128])
                    if "q2t" in dbg and h == 0:
                        v.tensor_copy(q2t_dbg[:], q2_s[:, 0:128])
                    v.memset(q2_s[:, 0:128], 1.0)
                    v.tensor_scalar(q2_s[:, 0:128], q2_s[:, 0:128],
                                    kmxf[:, h:h + 1], None, ALU.mult)
                    v.engine_nop().then_inc(sm("sVM"), 1)
                vm += 1

                @blk.tensor
                def _(t_, vm_=vm):
                    t_.wait_ge(sm("sVM"), vm_)
                    t_.transpose(ps_t[:, 0, 0:128], q2_s[:, 0:128], eye_s[:]
                                 ).then_inc(sm("sTM"), 1)
                tm += 1

                @blk.vector
                def _(v, tm_=tm, h=h):
                    v = _VDrain(v)
                    v.wait_ge(sm("sTM"), tm_)
                    v.reduce_max(kk_sb[0:1, h:h + 1], ps_t[0:1, 0, 0:128],
                                 axis=AX.X)
                    v.tensor_scalar(kk8[0:1, :], ones_r[0:1, 0:8],
                                    kk_sb[0:1, h:h + 1], None, ALU.mult)
                    if "pst" in dbg:
                        v.tensor_copy(pst_dbg[0:1, h, :], ps_t[0:1, 0, 0:128])
                    v.engine_nop().then_inc(sm("sVM"), 1)
                vm += 1

                @blk.tensor
                def _(t_, vm_=vm):
                    t_.wait_ge(sm("sVM"), vm_)
                    t_.matmul(ps_c[:, 0, 0:8], ones_r[:], kk8[:],
                              start=True, stop=True).then_inc(sm("sTM"), 1)
                tm += 1

                @blk.vector
                def _(v, tm_=tm, h=h):
                    v = _VDrain(v)
                    v.wait_ge(sm("sTM"), tm_)
                    v.tensor_copy(kmxf[:, h:h + 1], ps_c[:, 0, 0:1])
                    v.engine_nop().then_inc(sm("sVM"), 1)
                vm += 1

            for h in range(HL):
                @blk.vector
                def _(v, h=h):
                    v = _VDrain(v)
                    v.tensor_tensor(q2_s[:], qk_s[:, h, :], qk_s[:, h, :],
                                    ALU.mult)
                    v.engine_nop().then_inc(sm("sVM"), 1)
                vm += 1

                @blk.tensor
                def _(t_, vm_=vm):
                    t_.wait_ge(sm("sVM"), vm_)
                    for kt in range(64):
                        mm = t_.matmul(ps_c[:, 0, 8 * kt:8 * kt + 8],
                                       q2_s[:, 128 * kt:128 * (kt + 1)],
                                       ones8[:], start=(kt == 0), stop=True,
                                       skip_group_check=True)
                    mm.then_inc(sm("sTM"), 1)
                tm += 1

                @blk.vector
                def _(v, tm_=tm, h=h):
                    v = _VDrain(v)
                    v.wait_ge(sm("sTM"), tm_)
                    v.tensor_scalar(mh_sc[:], ps_c[:, 0, 0:512:8],
                                    kmxf[:, h:h + 1], None, ALU.mult)
                    v.engine_nop().then_inc(sm("sVM"), 1)
                vm += 1

                @blk.scalar
                def _(sc, vm_=vm, h=h):
                    sc.wait_ge(sm("sVM"), vm_)
                    sc.activation(negm[:, h, :], mh_sc[:], AF.Sqrt,
                                  scale=MARG / 128.0).then_inc(sm("sSM"), 1)
                smm += 1

                @blk.vector
                def _(v, smm_=smm, h=h):
                    v = _VDrain(v)
                    v.wait_ge(sm("sSM"), smm_)
                    v.tensor_scalar(negm[:, h, :], negm[:, h, :], -1.0, None,
                                    ALU.mult)
                    v.engine_nop().then_inc(sm("sVM"), 1)
                vm += 1
            M_vm_prep = vm

            dump("negm", negm[:], "sVM", M_vm_prep)
            dump("kmx", kmxf[:], "sVM", M_vm_prep)
            dump("kks", kk_sb[:], "sVM", M_vm_prep)
            dump("kk8d", kk8[:], "sVM", M_vm_prep)
            if "kms" in dbg:
                dump("kms", kms_dbg[:], "sVM", M_vm_prep)
                dump("psc", psc_dbg[:], "sVM", M_vm_prep)
                dump("q2s", q2s_dbg[:], "sVM", M_vm_prep)
            if "pst" in dbg:
                dump("pst", pst_dbg[:], "sVM", M_vm_prep)
            if "q2t" in dbg:
                dump("q2t", q2t_dbg[:], "sVM", M_vm_prep)
            dump("on8", ones8[:], "sVM", M_vm_prep)
            dump("eyeE", eye_s[:], "sVM", M_vm_prep)
            dump("eyeL", eye_s[:], "sDQ", NUNITS)
            dump("gupd", gup_s[:], "sVM", M_vm_prep)

            dn_dma_ct = [0]
            dn_ev_ct = [0]

            def dq_vec_unit(v, u):
                par = u % 2
                if u >= 2:
                    v.wait_ge(sm("sDQT"), 16 * (u - 1))
                if u < 11:
                    lt = u
                    srcfull = gup_s[:, lt, :]
                    wn = 256
                    scf = (lambda g, lt=lt: gscf[:, lt, g:g + 1])
                else:
                    ht, w0, wn = DN_CHUNKS[u - 11]
                    srcfull = dnp_s[:, ht, w0:w0 + wn]
                    g0 = (8 * w0) // 128
                    scf = (lambda g, ht=ht, g0=g0:
                           dscf[:, ht, g0 + g:g0 + g + 1])
                for sub in range((8 * wn + 1023) // 1024):
                    wsub = min(128, wn - 128 * sub)
                    n = 8 * wsub
                    src = srcfull[:, 128 * sub:128 * sub + wsub]
                    for j in range(8):
                        v.tensor_scalar(dq_mag[:, j:n:8], src, 4 * j, 7,
                                        ALU.logical_shift_right, ALU.bitwise_and)
                    v.tensor_scalar(dq_sg[:, 0:n], dq_mag[:, 0:n], 1, None,
                                    ALU.logical_shift_right)
                    v.tensor_copy(dq_a[:, 0:n], dq_sg[:, 0:n])
                    v.tensor_scalar(dq_mag[:, 0:n], dq_mag[:, 0:n], 1, None,
                                    ALU.bitwise_and)
                    v.tensor_copy(dq_b[:, 0:n], dq_mag[:, 0:n])
                    v.tensor_scalar(dq_cw[:, 0:n], dq_a[:, 0:n], 0.5, 0.5,
                                    ALU.mult, ALU.subtract)
                    v.tensor_tensor(dq_cw[:, 0:n], dq_cw[:, 0:n], dq_a[:, 0:n],
                                    ALU.mult)
                    v.tensor_scalar(dq_cw[:, 0:n], dq_cw[:, 0:n], 1.0, None,
                                    ALU.add)
                    v.tensor_scalar(dq_a[:, 0:n], dq_b[:, 0:n], 0.5, 1.0,
                                    ALU.mult, ALU.add)
                    v.tensor_tensor(dq_a[:, 0:n], dq_a[:, 0:n], dq_cw[:, 0:n],
                                    ALU.mult)
                    v.tensor_scalar(dq_b[:, 0:n], dq_b[:, 0:n], 0.5, None,
                                    ALU.mult)
                    v.tensor_scalar(dq_sg[:, 0:n], dq_sg[:, 0:n], 1, None,
                                    ALU.min)
                    v.tensor_copy(dq_cw[:, 0:n], dq_sg[:, 0:n])
                    v.tensor_tensor(dq_a[:, 0:n], dq_a[:, 0:n], dq_b[:, 0:n],
                                    ALU.subtract)
                    v.tensor_tensor(dq_a[:, 0:n], dq_a[:, 0:n], dq_cw[:, 0:n],
                                    ALU.mult)
                    v.tensor_tensor(dq_cw[:, 0:n], dq_a[:, 0:n], dq_b[:, 0:n],
                                    ALU.add)
                    for j in range(8):
                        v.tensor_scalar(dq_mag[:, j:n:8], src, 4 * j + 3, 1,
                                        ALU.logical_shift_right, ALU.bitwise_and)
                    v.tensor_copy(dq_a[:, 0:n], dq_mag[:, 0:n])
                    v.tensor_scalar(dq_a[:, 0:n], dq_a[:, 0:n], -2.0, 1.0,
                                    ALU.mult, ALU.add)
                    v.tensor_tensor(dq_cw[:, 0:n], dq_cw[:, 0:n], dq_a[:, 0:n],
                                    ALU.mult)
                    for gl in range((n + 127) // 128):
                        g = 8 * sub + gl
                        o0 = 1024 * sub + 128 * gl
                        o1 = min(1024 * sub + n, o0 + 128)
                        v.tensor_scalar(dq_w[:, par, o0:o1],
                                        dq_cw[:, 128 * gl:128 * gl + (o1 - o0)],
                                        scf(g), None, ALU.mult)
                v.engine_nop().then_inc(sm("sDQ"), 1)

            def dq_tr_unit(t_, u):
                par = u % 2
                nblk = 16 if u < 11 else (8 * DN_CHUNKS[u - 11][2]) // 128
                t_.wait_ge(sm("sDQ"), u + 1)
                for b in range(nblk):
                    gi = 16 * u + b
                    if gi >= 2:
                        t_.wait_ge(sm("sDQE"),
                                   16 * u if b < 2 else 16 * u + b - 1)
                    inc = 1 + (16 - nblk if b == nblk - 1 else 0)
                    t_.transpose(ps_t[:, gi % 2, 0:128],
                                 dq_w[:, par, 128 * b:128 * (b + 1)],
                                 eye_s[:]).then_inc(sm("sDQT"), inc)

            def dq_ev_unit(v, u):
                par = u % 2
                nblk = 16 if u < 11 else (8 * DN_CHUNKS[u - 11][2]) // 128
                for b in range(nblk):
                    gi = 16 * u + b
                    v.wait_ge(sm("sDQT"),
                              gi + 1 if b < nblk - 1 else 16 * (u + 1))
                    inc = 1 + (16 - nblk if b == nblk - 1 else 0)
                    if u < 11:
                        if u >= 1 and b == 0:
                            v.wait_ge(sm("sDQW"), 16 * u)
                        v.tensor_copy(dq_stg[:, 0, b:2048:16],
                                      ps_t[:, gi % 2, 0:128]
                                      ).then_inc(sm("sDQE"), inc)
                    else:
                        nb = dn_ev_ct[0]
                        if nb >= 4:
                            v.wait_ge(sm("sDQW2"), 16 * (nb - 3))
                        v.tensor_copy(dq_dnb[:, nb % 4, :],
                                      ps_t[:, gi % 2, 0:128]
                                      ).then_inc(sm("sDQE"), inc)
                        dn_ev_ct[0] += 1

            def dq_dma_unit(sy, u):
                par = u % 2
                if u < 11:
                    sy.wait_ge(sm("sDQE"), 16 * u + 16)
                    if u >= 1:
                        sy.wait_ge(sm("sDQW"), 16 * u)
                    sy.dma_start(
                        out=bass.AP(gu_in, u * 2048, [[11 * 2048, 128], [1, 2048]]),
                        in_=dq_stg[:, 0, :]).then_inc(sm("sDQW"), 16)
                else:
                    ht, w0, wn = DN_CHUNKS[u - 11]
                    nblk = (8 * wn) // 128
                    it0 = (8 * w0) // 128
                    ev_base = 16 * u
                    for b in range(nblk):
                        sy.wait_ge(sm("sDQE"),
                                   ev_base + b + 1 if b < nblk - 1
                                   else 16 * (u + 1))
                        if dn_dma_ct[0] >= 1:
                            sy.wait_ge(sm("sDQW2"), 16 * dn_dma_ct[0])
                        it = it0 + b
                        sy.dma_start(
                            out=bass.AP(dn_in, it * 128 * 256 + 128 * ht,
                                        [[256, 128], [1, 128]]),
                            in_=dq_dnb[:, dn_dma_ct[0] % 4, :]
                        ).then_inc(sm("sDQW2"), 16)
                        dn_dma_ct[0] += 1

            M_dqw = 16 * 11
            M_dqw2 = 16 * 88

            inj_v, inj_t, inj_e, inj_d = {}, {}, {}, {}
            for u in range(NUNITS):
                it0 = 2 + u * 3
                inj_v[it0] = u
                inj_t[it0 + 1] = u
                inj_e[it0 + 1] = u
                inj_d[it0 + 2] = u

            sSCn, sEXn, sRn, sCMn = ["sSC0", "sSC1"], ["sEX0", "sEX1"], \
                ["sR0", "sR1"], ["sCM0", "sCM1"]

            @blk.tensor
            def _(t_):
                t_.wait_ge(sm("sVM"), M_vm_prep)
                for t in range(64):
                    for h in range(HL):
                        if t > 0:
                            t_.wait_ge(sm(sRn[h]), t)
                            for kt in range(64):
                                mm = t_.matmul(ps_c[:, h, 8 * kt:8 * kt + 8],
                                               e_s[:, h, 128 * kt:128 * (kt + 1)],
                                               rb8[:, h, (t - 1) % 2, :],
                                               start=(t == 1 and kt == 0),
                                               stop=True,
                                               skip_group_check=True)
                            mm.then_inc(sm(sCMn[h]), 1)
                        for ch in range(16):
                            g = 16 * t + ch
                            if g >= 2:
                                t_.wait_ge(sm(sEXn[h]), g - 1)
                            t_.matmul(ps_s[:, h, ch % 2, :],
                                      qk_s[:, h, 128 * t:128 * (t + 1)],
                                      qk_s[:, 2 + h, 512 * ch:512 * (ch + 1)],
                                      start=True, stop=True
                                      ).then_inc(sm(sSCn[h]), 1)
                    if t in inj_t:
                        dq_tr_unit(t_, inj_t[t])
                for h in range(HL):
                    t_.wait_ge(sm(sRn[h]), 64)
                    for kt in range(64):
                        mm = t_.matmul(ps_c[:, h, 8 * kt:8 * kt + 8],
                                       e_s[:, h, 128 * kt:128 * (kt + 1)],
                                       rb8[:, h, 1, :],
                                       start=False, stop=True,
                                       skip_group_check=True)
                    mm.then_inc(sm(sCMn[h]), 1)
            M_cm = 64

            @blk.scalar
            def _(sc):
                for t in range(64):
                    for h in range(HL):
                        for ch in range(16):
                            g = 16 * t + ch
                            sc.wait_ge(sm(sSCn[h]), g + 1)
                            if t > 0 and ch == 0:
                                sc.wait_ge(sm(sCMn[h]), t)
                            sc.activation(e_s[:, h, 512 * ch:512 * (ch + 1)],
                                          ps_s[:, h, ch % 2, :], AF.Exp,
                                          bias=negm[:, h, t:t + 1], scale=RSCL,
                                          accum_out=zf[:, h, t % 2, ch:ch + 1]
                                          ).then_inc(sm(sEXn[h]), 1)

            @blk.vector
            def _(v):
                v = _VDrain(v)
                for t in range(64):
                    for h in range(HL):
                        v.wait_ge(sm(sEXn[h]), 16 * (t + 1))
                        if t >= 2:
                            v.wait_ge(sm(sCMn[h]), t - 1)
                        v.reduce_sum(zs[:, h, 0:1], zf[:, h, t % 2, :], axis=AX.X)
                        v.reciprocal(zs[:, h, 1:2], zs[:, h, 0:1])
                        v.tensor_copy(rb8[:, h, t % 2, 0:1], zs[:, h, 1:2]
                                      ).then_inc(sm(sRn[h]), 1)
                    if t in inj_v:
                        dq_vec_unit(v, inj_v[t])
                    if t in inj_e:
                        dq_ev_unit(v, inj_e[t])

            @blk.sync
            def _(sy):
                for t in range(64):
                    if t in inj_d:
                        dq_dma_unit(sy, inj_d[t])

            @blk.vector
            def _(v):
                v = _VDrain(v)
                for h in range(HL):
                    v.wait_ge(sm(sCMn[h]), M_cm)
                    v.tensor_copy(c_sb[:, h, :], ps_c[:, h, 0:512:8])
                v.tensor_copy(cbf[:], c_sb[:])
                v.engine_nop().then_inc(sm("sVA"), 1)

            @blk.tensor
            def _(t_, tm_=tm):
                t_.wait_ge(sm("sVA"), 1)
                t_.transpose(ps_t[0:64, 0, 0:128], cbf[:, 0, :], eye_s[:]
                             ).then_inc(sm("sTM"), 1)
                t_.transpose(ps_t[0:64, 1, 0:128], cbf[:, 1, :], eye_s[:]
                             ).then_inc(sm("sTM"), 1)
            tm += 2

            @blk.vector
            def _(v, tm_=tm):
                v = _VDrain(v)
                v.wait_ge(sm("sTM"), tm_)
                v.tensor_copy(c_tb[:, 0, :], ps_t[0:64, 0, 0:128])
                v.tensor_copy(c_tb[:, 1, :], ps_t[0:64, 1, 0:128])
                v.engine_nop().then_inc(sm("sVA"), 1)
            M_va_c = 2

            @blk.scalar
            def _(sc):
                sc.wait_ge(sm("sVA"), M_va_c)
                sc.dma_start(
                    out=bass.AP(c_in, 0, [[128, 64], [8192, HL], [1, 128]]),
                    in_=c_tb[:]).then_inc(sm("sCW"), 16)
            M_cw = 16

            @blk.gpsimd
            def _(g):
                g.wait_ge(sm("sDQW"), M_dqw)
                g.collective_compute(
                    "AllGather", ALU.bypass, replica_groups=[list(range(NC))],
                    ins=[gu_in.ap().opt()], outs=[gu_out.ap().opt()],
                ).then_inc(sm("sCC"), 1)
                g.wait_ge(sm("sDQW2"), M_dqw2)
                g.collective_compute(
                    "AllGather", ALU.bypass, replica_groups=[list(range(NC))],
                    ins=[dn_in.ap().opt()], outs=[dn_out.ap().opt()],
                ).then_inc(sm("sCC"), 1)
                g.wait_ge(sm("sCW"), M_cw)
                g.collective_compute(
                    "AllGather", ALU.bypass, replica_groups=[list(range(NC))],
                    ins=[c_in.ap().opt()], outs=[c_out.ap().opt()],
                ).then_inc(sm("sCC"), 1)
            M_cc_gu, M_cc_dn, M_cc_c = 2, 3, 4

            dump("wgu", gu_out.ap(), "sCC", M_cc_gu)
            dump("wdn", dn_out.ap(), "sCC", M_cc_dn)
            dump("scl1", gscf[:], "sDQ", NUNITS)
            dump("dsc1", dscf[:], "sDQ", NUNITS)
            dump("cga", c_out.ap(), "sCC", M_cc_c)
        stk_qk.close()

        stk_ln2t = ExitStack()
        with ExitStack() as p6:
            P = p6.enter_context
            ca_bf = P(nc.sbuf_tensor("ca_bf", [16, SC], BF16))
            ct_s = P(nc.sbuf_tensor("ct_s", [128, 8, 16], F32))
            attt = P(nc.sbuf_tensor("attt", [128, 16, SC], BF16))
            xr_s = P(nc.sbuf_tensor("xr_s", [128, 8, H], F32))
            wo_b = P(nc.sbuf_tensor("wo_b", [128, 1, 16, 512], BF16))
            ps6 = P(nc.psum_tensor("ps6", [128, 2, 512], F32))
            ps6b = P(nc.psum_tensor("ps6b", [128, 2, 1024], BF16))

            TA_ct0, TA_attt0, TA_wo0, TA_ln2t0 = 0, 8, 136, 168
            VA_base = M_va_c
            VA_ct0 = VA_base
            VA_att = VA_ct0 + 8 + 1
            VA_attt0 = VA_att
            VA_x20 = VA_attt0 + 128
            VA_var2 = VA_x20 + 32 + 1
            VA_ln2 = VA_var2 + 1
            VA_ln2t0 = VA_ln2
            VA_ln2t = VA_ln2t0 + 128

            @blk.sync
            def _(sy):
                sy.wait_ge(sm("sCM0"), 64)
                sy.wait_ge(sm("sCM1"), 64)
                sy.wait_ge(sm("sCW"), M_cw)
                if n_dbg[0]:
                    sy.wait_ge(sm("sDBG"), n_dbg[0])
                for t in range(8):
                    sy.dma_start(out=xr_s[:, t, :],
                                 in_=x_d.ap()[128 * t:128 * (t + 1), :]
                                 ).then_inc(sm("sXR"), 16)
            M_xr = 128

            @blk.gpsimd
            def _(g):
                g.wait_ge(sm("sCC"), M_cc_c)
                if n_dbg[0]:
                    g.wait_ge(sm("sDBG"), n_dbg[0])
                with g.register("coff") as coff:
                    g.reg_load(coff, meta_s[0:1, 0:1])
                    g.dma_start(out=ca_bf[:],
                                in_=bass.AP(c_out, coff, [[S, 16], [1, SC]])
                                ).then_inc(sm("sCA"), 16)

            @blk.tensor
            def _(t_):
                t_.wait_ge(sm("sCA"), 16)
                for st_ in range(8):
                    if st_ >= 2:
                        t_.wait_ge(sm("sVA"), VA_ct0 + st_ - 1)
                    t_.transpose(ps6b[:, st_ % 2, 0:16],
                                 ca_bf[:, 128 * st_:128 * (st_ + 1)],
                                 eye_s[0:16, 0:16]).then_inc(sm("sTA"), 1)

            @blk.vector
            def _(v):
                v = _VDrain(v)
                for st_ in range(8):
                    v.wait_ge(sm("sTA"), TA_ct0 + st_ + 1)
                    v.tensor_copy(ct_s[:, st_, :], ps6b[:, st_ % 2, 0:16]
                                  ).then_inc(sm("sVA"), 1)
                for st_ in range(8):
                    for hh in range(16):
                        v.tensor_scalar(v_s[:, st_, 128 * hh:128 * (hh + 1)],
                                        v_s[:, st_, 128 * hh:128 * (hh + 1)],
                                        ct_s[:, st_, hh:hh + 1], None, ALU.mult)
                v.engine_nop().then_inc(sm("sVA"), 1)

            @blk.tensor
            def _(t_):
                t_.wait_ge(sm("sVA"), VA_att)
                for i in range(128):
                    st_, ft = divmod(i, 16)
                    if i >= 2:
                        t_.wait_ge(sm("sVA"), VA_attt0 + i - 1)
                    t_.transpose(ps6b[:, i % 2, 0:128],
                                 v_s[:, st_, 128 * ft:128 * (ft + 1)],
                                 eye_s[:]).then_inc(sm("sTA"), 1)

            @blk.vector
            def _(v):
                v = _VDrain(v)
                for i in range(128):
                    st_, ft = divmod(i, 16)
                    v.wait_ge(sm("sTA"), TA_attt0 + i + 1)
                    v.tensor_copy(attt[:, ft, 128 * st_:128 * (st_ + 1)],
                                  ps6b[:, i % 2, 0:128]).then_inc(sm("sVA"), 1)
            lnb2 = v_s
            ln2t = ln1t
            bns2 = P(nc.sbuf_tensor("bns2", [128, 8, 4, 6], F32))
            mv2 = P(nc.sbuf_tensor("mv2", [128, 8, 2], F32))
            st2 = P(nc.sbuf_tensor("st2", [128, 6, 8], F32))

            @blk.gpsimd
            def _(g):
                g.wait_ge(sm("sCM0"), 64)
                g.wait_ge(sm("sCM1"), 64)
                for oc in range(4):
                    if oc >= 1:
                        g.wait_ge(sm("sTA"), TA_wo0 + 8 * oc)
                        g.wait_ge(sm("sWO"), 256 * oc)
                    for t in range(16):
                        g.dma_start(out=wo_b[:, 0, t, :],
                                    in_=wo_d.ap()[128 * t:128 * (t + 1),
                                                  512 * oc:512 * (oc + 1)]
                                    ).then_inc(sm("sWO"), 16)

            @blk.tensor
            def _(t_):
                t_.wait_ge(sm("sVA"), VA_attt0 + 128)
                for oc in range(4):
                    t_.wait_ge(sm("sWO"), 256 * (oc + 1))
                    for st_ in range(8):
                        gidx = oc * 8 + st_
                        if gidx >= 2:
                            t_.wait_ge(sm("sVA"), VA_x20 + gidx - 1)
                        for ft in range(16):
                            mm = t_.matmul(ps6[:, gidx % 2, :],
                                           attt[:, ft, 128 * st_:128 * (st_ + 1)],
                                           wo_b[:, 0, ft, :],
                                           start=(ft == 0), stop=(ft == 15))
                        mm.then_inc(sm("sTA"), 1)

            @blk.vector
            def _(v):
                v = _VDrain(v)
                v.wait_ge(sm("sXR"), M_xr)
                for gidx in range(32):
                    oc, st_ = divmod(gidx, 8)
                    v.wait_ge(sm("sTA"), TA_wo0 + gidx + 1)
                    v.tensor_tensor(xr_s[:, st_, 512 * oc:512 * (oc + 1)],
                                    ps6[:, gidx % 2, :],
                                    xr_s[:, st_, 512 * oc:512 * (oc + 1)],
                                    ALU.add).then_inc(sm("sVA"), 1)

            @blk.vector
            def _(v):
                v = _VDrain(v)
                v.wait_ge(sm("sVA"), VA_x20 + 32)
                for t in range(8):
                    for k in range(4):
                        v.bn_stats(bns2[:, t, k, :],
                                   xr_s[:, t, 512 * k:512 * (k + 1)])
                    v.bn_aggr(mv2[:, t, :], bns2[:, t, :, :])
                v.tensor_scalar(st2[:, 0, :], mv2[:, :, 0], -1.0, None, ALU.mult)
                v.tensor_scalar(st2[:, 1, :], mv2[:, :, 1], LN_EPS, None, ALU.add)
                v.engine_nop().then_inc(sm("sVA"), 1)

            @blk.scalar
            def _(sc):
                sc.wait_ge(sm("sVA"), VA_var2)
                sc.activation(st2[:, 2, :], st2[:, 1, :], AF.Sqrt
                              ).then_inc(sm("sSA"), 1)
            M_sa_sq2 = 1

            @blk.vector
            def _(v):
                v = _VDrain(v)
                v.wait_ge(sm("sSA"), M_sa_sq2)
                _newton_rstd(v, st2, 1, 2, 3, 4, 5)
                for t in range(8):
                    v.tensor_scalar(lnb2[:, t, :], xr_s[:, t, :],
                                    st2[:, 0, t:t + 1], st2[:, 5, t:t + 1],
                                    ALU.add, ALU.mult)
                v.engine_nop().then_inc(sm("sVA"), 1)

            @blk.tensor
            def _(t_):
                t_.wait_ge(sm("sVA"), VA_ln2)
                for i in range(128):
                    st_, ft = divmod(i, 16)
                    if i >= 2:
                        t_.wait_ge(sm("sVA"), VA_ln2t0 + i - 1)
                    t_.transpose(ps6b[:, i % 2, 0:128],
                                 lnb2[:, st_, 128 * ft:128 * (ft + 1)],
                                 eye_s[:]).then_inc(sm("sTA"), 1)

            @blk.vector
            def _(v):
                v = _VDrain(v)
                for i in range(128):
                    st_, ft = divmod(i, 16)
                    v.wait_ge(sm("sTA"), TA_ln2t0 + i + 1)
                    v.tensor_copy(ln2t[:, ft, 128 * st_:128 * (st_ + 1)],
                                  ps6b[:, i % 2, 0:128]).then_inc(sm("sVA"), 1)

            @blk.sync
            def _(sy):
                sy.wait_ge(sm("sVA"), VA_ln2t)
                for t in range(8):
                    sy.dma_start(out=x2_h.ap()[128 * t:128 * (t + 1), :],
                                 in_=xr_s[:, t, :]).then_inc(sm("sX2S"), 16)
            M_x2s = 128

            dump("x2", x2_h.ap(), "sX2S", M_x2s)

        with ExitStack() as p8:
            P = p8.enter_context
            hact = P(nc.sbuf_tensor("hact", [128, 44, SC], BF16))
            ps_gu = P(nc.psum_tensor("ps_gu", [128, 8, 512], F32))
            stk_gu = ExitStack()
            guw = stk_gu.enter_context(nc.sbuf_tensor("guw", [128, 4, 2048], BF16))
            tmpg = stk_gu.enter_context(nc.sbuf_tensor("tmpg", [128, 2, 2, 512], BF16))

            TA_gu0 = 296
            TA_dn0 = TA_gu0 + 176
            VA_h0 = VA_ln2t

            def gu_src(ot):
                r, lt = divmod(ot, 11)
                return bass.AP(gu_out, r * (128 * 11 * 2048) + lt * 2048,
                               [[11 * 2048, 128], [1, 2048]])

            @blk.gpsimd
            def _(g):
                g.wait_ge(sm("sCC"), M_cc_gu)
                g.wait_ge(sm("sVA"), VA_ln2t)
                g.wait_ge(sm("sX2S"), M_x2s)
                for j in range(44):
                    if j >= 1:
                        g.wait_ge(sm("sTA"), TA_gu0 + max(0, 4 * (j - 1)))
                        g.wait_ge(sm("sGWg"), 16 * j)
                    g.dma_start(out=guw[:, (2 * j) % 4, :], in_=gu_src(j)
                                ).then_inc(sm("sGWg"), 16)

            @blk.sync
            def _(sy):
                sy.wait_ge(sm("sCC"), M_cc_gu)
                sy.wait_ge(sm("sVA"), VA_ln2t)
                sy.wait_ge(sm("sX2S"), M_x2s)
                for j in range(44):
                    if j >= 1:
                        sy.wait_ge(sm("sTA"), TA_gu0 + max(0, 4 * (j - 1)))
                        sy.wait_ge(sm("sGWu"), 16 * j)
                    sy.dma_start(out=guw[:, (2 * j + 1) % 4, :], in_=gu_src(44 + j)
                                 ).then_inc(sm("sGWu"), 16)

            @blk.tensor
            def _(t_):
                t_.wait_ge(sm("sVA"), VA_ln2t)
                for j in range(44):
                    if j >= 2:
                        t_.wait_ge(sm("sVA"), VA_h0 + 2 * (j - 1))
                    t_.wait_ge(sm("sGWg"), 16 * (j + 1))
                    t_.wait_ge(sm("sGWu"), 16 * (j + 1))
                    for half in range(2):
                        for scn in range(2):
                            bank = 4 * (j % 2) + 2 * half + scn
                            for ht in range(16):
                                mm = t_.matmul(ps_gu[:, bank, :],
                                               guw[:, (2 * j + half) % 4, ht:2048:16],
                                               ln2t[:, ht, 512 * scn:512 * (scn + 1)],
                                               start=(ht == 0), stop=(ht == 15))
                            mm.then_inc(sm("sTA"), 1)

            @blk.scalar
            def _(sc):
                for j in range(44):
                    for scn in range(2):
                        sc.wait_ge(sm("sTA"), TA_gu0 + 4 * j + scn + 1)
                        if j >= 2:
                            sc.wait_ge(sm("sVA"), VA_h0 + 2 * (j - 1))
                        sc.activation(tmpg[:, j % 2, scn, :],
                                      ps_gu[:, 4 * (j % 2) + scn, :], AF.Silu
                                      ).then_inc(sm("sSA"), 1)
            SA_gu0 = M_sa_sq2

            @blk.vector
            def _(v):
                v = _VDrain(v)
                for j in range(44):
                    for scn in range(2):
                        v.wait_ge(sm("sSA"), SA_gu0 + 2 * j + scn + 1)
                        v.wait_ge(sm("sTA"), TA_gu0 + 4 * j + 2 + scn + 1)
                        v.tensor_tensor(hact[:, j, 512 * scn:512 * (scn + 1)],
                                        tmpg[:, j % 2, scn, :],
                                        ps_gu[:, 4 * (j % 2) + 2 + scn, :],
                                        ALU.mult).then_inc(sm("sVA"), 1)
            VA_hact = VA_h0 + 88

            dump("hact", hact[:], "sVA", VA_hact)

            stk_gu.close()
            dnw2 = P(nc.sbuf_tensor("dnw2", [128, 12, 512], BF16))
            lnt_flat = ln1t[:].rearrange("p a b -> p (a b)")
            y_bf = v_s

            def dnw_ap(it, c0=0, c1=512):
                if it < 32:
                    return lnt_flat[:, 512 * it + c0:512 * it + c1]
                return dnw2[:, it - 32, c0:c1]
            x2r = P(nc.sbuf_tensor("x2r", [128, 2, H], F32))
            bns3 = P(nc.sbuf_tensor("bns3", [128, 8, 4, 6], F32))
            mv3 = P(nc.sbuf_tensor("mv3", [128, 8, 2], F32))
            st3 = P(nc.sbuf_tensor("st3", [128, 6, 8], F32))

            @blk.sync
            def _(sy):
                sy.wait_ge(sm("sCC"), M_cc_dn)
                sy.wait_ge(sm("sVA"), VA_hact)
                for hc in range(4):
                    if hc >= 1:
                        sy.wait_ge(sm("sTA"), TA_dn0 + 8 * hc)
                        sy.wait_ge(sm("sDW"), 1408 * hc)
                    for it in range(44):
                        for rr in range(2):
                            sy.dma_start(
                                out=dnw_ap(it, 256 * rr, 256 * (rr + 1)),
                                in_=dn_out.ap()[2 * hc + rr][it]
                            ).then_inc(sm("sDW"), 16)

            @blk.tensor
            def _(t_):
                t_.wait_ge(sm("sVA"), VA_hact)
                for hc in range(4):
                    t_.wait_ge(sm("sDW"), 1408 * (hc + 1))
                    for st_ in range(8):
                        gi = 8 * hc + st_
                        if gi >= 2:
                            t_.wait_ge(sm("sYEV"), gi - 1)
                        for it in range(44):
                            mm = t_.matmul(ps_gu[:, gi % 2, :],
                                           hact[:, it, 128 * st_:128 * (st_ + 1)],
                                           dnw_ap(it),
                                           start=(it == 0), stop=(it == 43))
                        mm.then_inc(sm("sTA"), 1)

            @blk.vector
            def _(v):
                v = _VDrain(v)
                for gi in range(32):
                    hc, st_ = divmod(gi, 8)
                    v.wait_ge(sm("sTA"), TA_dn0 + gi + 1)
                    v.tensor_copy(y_bf[:, st_, 512 * hc:512 * (hc + 1)],
                                  ps_gu[:, gi % 2, :]).then_inc(sm("sYEV"), 1)
            M_yev = 32

            @blk.vector
            def _(v):
                v = _VDrain(v)
                v.wait_ge(sm("sYEV"), M_yev)
                for t in range(8):
                    for k in range(4):
                        v.bn_stats(bns3[:, t, k, :],
                                   y_bf[:, t, 512 * k:512 * (k + 1)])
                    v.bn_aggr(mv3[:, t, :], bns3[:, t, :, :])
                v.tensor_tensor(st3[:, 0, :], mv3[:, :, 0], mv3[:, :, 0],
                                ALU.mult)
                v.tensor_tensor(st3[:, 1, :], st3[:, 0, :], mv3[:, :, 1],
                                ALU.add)
                v.tensor_scalar(st3[:, 1, :], st3[:, 1, :], RMS_EPS, None,
                                ALU.add)
                v.engine_nop().then_inc(sm("sVA"), 1)
            VA_rms = VA_hact + 1

            @blk.scalar
            def _(sc):
                sc.wait_ge(sm("sVA"), VA_rms)
                sc.activation(st3[:, 2, :], st3[:, 1, :], AF.Sqrt
                              ).then_inc(sm("sSA"), 1)
            SA_rms = SA_gu0 + 88 + 1

            @blk.vector
            def _(v):
                v = _VDrain(v)
                v.wait_ge(sm("sSA"), SA_rms)
                _newton_rstd(v, st3, 1, 2, 3, 4, 5)
                v.engine_nop().then_inc(sm("sVA"), 1)
            VA_rstd = VA_rms + 1

            @blk.sync
            def _(sy):
                sy.wait_ge(sm("sX2S"), M_x2s)
                for t in range(8):
                    if t >= 1:
                        sy.wait_ge(sm("sVA"), VA_rstd + max(0, t - 1))
                        sy.wait_ge(sm("sX2R"), 16 * t)
                        sy.wait_ge(sm("sOUT"), 16 * max(0, t - 1))
                    sy.dma_start(out=x2r[:, t % 2, :],
                                 in_=x2_h.ap()[128 * t:128 * (t + 1), :]
                                 ).then_inc(sm("sX2R"), 16)

            @blk.vector
            def _(v):
                v = _VDrain(v)
                v.wait_ge(sm("sVA"), VA_rstd)
                for t in range(8):
                    v.wait_ge(sm("sX2R"), 16 * max(t + 1, 2))
                    if t >= 2:
                        v.wait_ge(sm("sOUT"), 16 * (t - 1))
                    v.scalar_tensor_tensor(x2r[:, t % 2, :], y_bf[:, t, :],
                                           st3[:, 5, t:t + 1], x2r[:, t % 2, :],
                                           ALU.mult, ALU.add
                                           ).then_inc(sm("sVA"), 1)
            VA_out0 = VA_rstd

            @blk.scalar
            def _(sc):
                for t in range(8):
                    sc.wait_ge(sm("sVA"), VA_out0 + t + 1)
                    if t >= 1:
                        sc.wait_ge(sm("sOUT"), 16 * t)
                    sc.dma_start(out=y_d.ap()[128 * t:128 * (t + 1), :],
                                 in_=x2r[:, t % 2, :]).then_inc(sm("sOUT"), 16)

            @blk.sync
            def _(sy):
                sy.wait_ge(sm("sOUT"), 128)
                if n_dbg[0]:
                    sy.wait_ge(sm("sDBG"), n_dbg[0])

    return nc, dbg


def _host_inputs(x_in, w_qkv_in, w_o_in, gup_in, gsc_in, dnp_in, dsc_in):
    x = np.ascontiguousarray(np.asarray(x_in, dtype=np.float32).reshape(S, H))
    wqkv = np.asarray(w_qkv_in, dtype=np.float32)
    wo = np.ascontiguousarray(np.asarray(w_o_in, dtype=np.float32))
    gup = np.asarray(gup_in, dtype=np.int32)
    gsc = np.asarray(gsc_in, dtype=np.float16)
    dnp = np.asarray(dnp_in, dtype=np.int32)
    dsc = np.asarray(dsc_in, dtype=np.float16)
    wv = np.ascontiguousarray(wqkv[:, 2 * H:3 * H])
    eye = np.eye(128, dtype=ml_dtypes.bfloat16)
    in_maps = []
    for c in range(NC):
        cols = np.r_[256 * c:256 * c + 256, H + 256 * c:H + 256 * c + 256]
        in_maps.append(dict(
            x=np.ascontiguousarray(x[SC * c:SC * (c + 1)]),
            wqk=np.ascontiguousarray(wqkv[:, cols]),
            wv=wv,
            wo=wo,
            gup=np.ascontiguousarray(gup[GU_SH * c:GU_SH * (c + 1)]),
            gsc=np.ascontiguousarray(gsc[:, GU_SH * c:GU_SH * (c + 1)].T),
            dnp=np.ascontiguousarray(dnp[DN_SH * c:DN_SH * (c + 1)]),
            dsc=np.ascontiguousarray(dsc[:, DN_SH * c:DN_SH * (c + 1)].T),
            eye=eye,
            meta=np.array([[SC * c, c]], dtype=np.int32),
        ))
    return in_maps


def kernel(L0_attn_x, L0_attn_w_qkv, L0_attn_w_o, L0_ffn_gate_up_packed,
           L0_ffn_gate_up_scales, L0_ffn_down_packed, L0_ffn_down_scales):
    in_maps = _host_inputs(L0_attn_x, L0_attn_w_qkv, L0_attn_w_o,
                           L0_ffn_gate_up_packed, L0_ffn_gate_up_scales,
                           L0_ffn_down_packed, L0_ffn_down_scales)
    nc, dbg = build_nc(_DEBUG)
    res = run_bass_kernel_spmd(nc, in_maps, core_ids=list(range(NC)))
    if _DEBUG:
        kernel._last = res
    out = np.concatenate([res.results[c]["y"] for c in range(NC)], axis=0)
    return out.reshape(1, S, H).astype(np.float32)


# version 05 (final)
# speedup vs baseline: 1.0443x; 1.0443x over previous
import numpy as np
import ml_dtypes
from contextlib import ExitStack

import concourse.bass as bass
import concourse.mybir as mybir
from concourse.bass_utils import run_bass_kernel_spmd

F32 = mybir.dt.float32
BF16 = mybir.dt.bfloat16
F16 = mybir.dt.float16
I32 = mybir.dt.int32
AF = mybir.ActivationFunctionType
ALU = mybir.AluOpType
AX = mybir.AxisListType

NC = 8
S, H, SC = 8192, 2048, 1024
NH, D, HL = 16, 128, 2
IFF, OFF = 5632, 11264
GU_SH, DN_SH = OFF // NC, H // NC
RSCL = float(1.0 / np.sqrt(128.0))
MARG = 1.0816
LN_EPS, RMS_EPS = 1e-5, 1e-6

_DEBUG = set()

SEM_NAMES = (
    "sIN", "sGIN", "sLNW", "sCC", "sRHS", "sRHS2", "sWV", "sWO",
    "sT1", "sVE", "sTV", "sEVV", "sTQ", "sEVQ", "sTM", "sVM", "sSM",
    "sKW", "sKB",
    "sSC0", "sSC1", "sEX0", "sEX1", "sR0", "sR1", "sCM0", "sCM1",
    "sDQ", "sDQT", "sDQE", "sDQW", "sDQW2",
    "sCW", "sCA", "sTA", "sVA", "sSA", "sXR",
    "sGWg", "sGWu", "sDW", "sYEV", "sX2S", "sX2R", "sOUT", "sDBG",
)

DN_CHUNKS = [(0, 0, 256), (0, 256, 256), (0, 512, 192),
             (1, 0, 256), (1, 256, 256), (1, 512, 192)]
NUNITS = 11 + len(DN_CHUNKS)


def _newton_rstd(v, st, vcol, sqcol, r0col, t4col, outcol):
    v.reciprocal(st[:, r0col, :], st[:, sqcol, :])
    v.tensor_tensor(st[:, t4col, :], st[:, r0col, :], st[:, r0col, :], ALU.mult)
    v.tensor_tensor(st[:, t4col, :], st[:, t4col, :], st[:, vcol, :], ALU.mult)
    v.tensor_scalar(st[:, t4col, :], st[:, t4col, :], -0.5, 1.5, ALU.mult, ALU.add)
    v.tensor_tensor(st[:, outcol, :], st[:, r0col, :], st[:, t4col, :], ALU.mult)


class _VDrain:

    _OPS = {"tensor_scalar", "tensor_tensor", "tensor_copy", "tensor_add",
            "tensor_sub", "tensor_mul", "tensor_max", "reduce_sum",
            "reduce_max", "tensor_reduce", "bn_stats", "bn_aggr", "select",
            "reciprocal", "memset", "scalar_tensor_tensor", "tensor_relu",
            "tensor_scalar_add", "tensor_scalar_sub", "tensor_scalar_mul"}

    def __init__(self, eng):
        self._e = eng

    def __getattr__(self, n):
        f = getattr(self._e, n)
        if n in self._OPS:
            def g(*a, **k):
                r = f(*a, **k)
                self._e.drain()
                return r
            return g
        return f


def build_nc(debug=()):
    debug = set(debug)
    nc = bass.Bass("TRN2", target_bir_lowering=False,
                  dynamic_dma_scratch_size=16384)

    x_d = nc.dram_tensor("x", [SC, H], F32, kind="ExternalInput")
    wqk_d = nc.dram_tensor("wqk", [H, 4 * D], F32, kind="ExternalInput")
    wv_d = nc.dram_tensor("wv", [H, H], F32, kind="ExternalInput")
    wo_d = nc.dram_tensor("wo", [H, H], F32, kind="ExternalInput")
    gup_d = nc.dram_tensor("gup", [GU_SH, 256], I32, kind="ExternalInput")
    gsc_d = nc.dram_tensor("gsc", [GU_SH, 16], F16, kind="ExternalInput")
    dnp_d = nc.dram_tensor("dnp", [DN_SH, 704], I32, kind="ExternalInput")
    dsc_d = nc.dram_tensor("dsc", [DN_SH, 44], F16, kind="ExternalInput")
    eye_d = nc.dram_tensor("eye", [128, 128], BF16, kind="ExternalInput")
    meta_d = nc.dram_tensor("meta", [1, 2], I32, kind="ExternalInput")
    y_d = nc.dram_tensor("y", [SC, H], F32, kind="ExternalOutput")

    ln_in = nc.dram_tensor("cc_ln_in", [H, SC], BF16)
    ln_outA = nc.dram_tensor("cc_ln_outA", [NC, H // 2, SC], BF16)
    ln_outB = nc.dram_tensor("cc_ln_outB", [NC, H // 2, SC], BF16)
    c_in = nc.dram_tensor("cc_c_in", [HL, 64, 128], BF16)
    c_out = nc.dram_tensor("cc_c_out", [NC, HL, 64, 128], BF16)
    gu_in = nc.dram_tensor("cc_gu_in", [128, 11, 2048], BF16)
    gu_out = nc.dram_tensor("cc_gu_out", [NC, 128, 11, 2048], BF16)
    dn_in = nc.dram_tensor("cc_dn_in", [44, 128, 256], BF16)
    dn_out = nc.dram_tensor("cc_dn_out", [NC, 44, 128, 256], BF16)
    x2_h = nc.dram_tensor("x2_h", [SC, H], F32)

    dbg = {}

    def dbg_out(name, shape, dt):
        if name in debug:
            dbg[name] = nc.dram_tensor("d_" + name, shape, dt,
                                       kind="ExternalOutput")
    dbg_out("qk", [128, 4, S], BF16)
    dbg_out("v", [128, 8, H], BF16)
    dbg_out("negm", [128, HL, 64], F32)
    dbg_out("cga", [NC, HL, 64, 128], BF16)
    dbg_out("x2", [SC, H], F32)
    dbg_out("wgu", [NC, 128, 11, 2048], BF16)
    dbg_out("wdn", [NC, 44, 128, 256], BF16)
    dbg_out("hact", [128, 44, SC], BF16)
    dbg_out("scl0", [128, 11, 16], F32)
    dbg_out("scl1", [128, 11, 16], F32)
    dbg_out("dsc1", [128, 2, 44], F32)
    dbg_out("kmx", [128, HL], F32)
    dbg_out("kks", [1, 2], F32)
    dbg_out("kk8d", [1, 8], BF16)
    dbg_out("kms", [128, 2], F32)
    dbg_out("psc", [128, 2, 64], F32)
    dbg_out("pst", [1, 2, 128], F32)
    dbg_out("q2s", [128, 64], BF16)
    dbg_out("on8", [128, 8], BF16)
    dbg_out("eyeE", [128, 128], BF16)
    dbg_out("eyeL", [128, 128], BF16)
    for _n in ("ey0", "ey1", "ey2", "ey3", "ey4"):
        dbg_out(_n, [128, 128], BF16)
    dbg_out("gupd", [128, 11, 256], I32)
    dbg_out("q2t", [128, 128], BF16)

    n_dbg = [0]

    with ExitStack() as top:
        E = top.enter_context
        blk = E(nc.Block())
        sems = {n: E(nc.semaphore(n)) for n in SEM_NAMES}

        def sm(n):
            return sems[n]

        def dump(name, src_ap, wait_sem, wait_val):
            if name not in dbg:
                return
            @blk.gpsimd
            def _(g):
                g.wait_ge(sm(wait_sem), wait_val)
                g.dma_start(out=dbg[name].ap(), in_=src_ap
                            ).then_inc(sm("sDBG"), 16)
            n_dbg[0] += 16

        eye_s = E(nc.sbuf_tensor("eye_s", [128, 128], BF16))
        ones8 = E(nc.sbuf_tensor("ones8", [128, 8], BF16))
        meta_s = E(nc.sbuf_tensor("meta_s", [1, 2], I32))
        gup_s = E(nc.sbuf_tensor("gup_s", [128, 11, 256], I32))
        gscf = E(nc.sbuf_tensor("gscf", [128, 11, 16], F32))
        dnp_s = E(nc.sbuf_tensor("dnp_s", [128, 2, 704], I32))
        dscf = E(nc.sbuf_tensor("dscf", [128, 2, 44], F32))
        ln1t = E(nc.sbuf_tensor("lnt", [128, 16, SC], BF16))
        v_s = E(nc.sbuf_tensor("v_s", [128, 8, H], BF16))

        ph1 = ExitStack()
        if True:
            P1 = ph1.enter_context
            gsc_s = P1(nc.sbuf_tensor("gsc_s", [128, 11, 16], F16))
            dsc_s = P1(nc.sbuf_tensor("dsc_s", [128, 2, 44], F16))
            x_s = P1(nc.sbuf_tensor("x_s", [128, 8, H], F32))
            lnbuf = P1(nc.sbuf_tensor("lnbuf", [128, 8, H], BF16))
            bns = P1(nc.sbuf_tensor("bns", [128, 8, 4, 6], F32))
            mv = P1(nc.sbuf_tensor("mv", [128, 8, 2], F32))
            st1 = P1(nc.sbuf_tensor("st1", [128, 6, 8], F32))
            pst = P1(nc.psum_tensor("pst", [128, 2, 1024], BF16))

            @blk.sync
            def _(sy):
                sy.dma_start(out=eye_s[:], in_=eye_d.ap()).then_inc(sm("sIN"), 16)
                sy.dma_start(out=meta_s[:], in_=meta_d.ap()).then_inc(sm("sIN"), 16)
                for t in range(8):
                    sy.dma_start(out=x_s[:, t, :],
                                 in_=x_d.ap()[128 * t:128 * (t + 1), :]
                                 ).then_inc(sm("sIN"), 16)
            M_in = 16 * 10

            @blk.gpsimd
            def _(g):
                for t in range(11):
                    g.dma_start(out=gup_s[:, t, :],
                                in_=gup_d.ap()[128 * t:128 * (t + 1), :]
                                ).then_inc(sm("sGIN"), 16)
                for t in range(11):
                    g.dma_start(out=gsc_s[:, t, :],
                                in_=gsc_d.ap()[128 * t:128 * (t + 1), :]
                                ).then_inc(sm("sGIN"), 16)
                for t in range(2):
                    g.dma_start(out=dnp_s[:, t, :],
                                in_=dnp_d.ap()[128 * t:128 * (t + 1), :]
                                ).then_inc(sm("sGIN"), 16)
                for t in range(2):
                    g.dma_start(out=dsc_s[:, t, :],
                                in_=dsc_d.ap()[128 * t:128 * (t + 1), :]
                                ).then_inc(sm("sGIN"), 16)
            M_gin_packed = 16 * 26

            @blk.vector
            def _(v):
                v = _VDrain(v)
                v.memset(ones8[:], 1.0)
                v.wait_ge(sm("sGIN"), M_gin_packed)
                v.tensor_copy(gscf[:], gsc_s[:])
                v.tensor_copy(dscf[:], dsc_s[:])
                v.wait_ge(sm("sIN"), M_in)
                for t in range(8):
                    for k in range(4):
                        v.bn_stats(bns[:, t, k, :], x_s[:, t, 512 * k:512 * (k + 1)])
                    v.bn_aggr(mv[:, t, :], bns[:, t, :, :])
                v.tensor_scalar(st1[:, 0, :], mv[:, :, 0], -1.0, None, ALU.mult)
                v.tensor_scalar(st1[:, 1, :], mv[:, :, 1], LN_EPS, None, ALU.add)
                v.engine_nop().then_inc(sm("sVE"), 1)
            M_var1 = 1

            @blk.scalar
            def _(sc):
                sc.wait_ge(sm("sVE"), M_var1)
                sc.activation(st1[:, 2, :], st1[:, 1, :], AF.Sqrt
                              ).then_inc(sm("sSM"), 1)
            M_sq1 = 1

            @blk.vector
            def _(v):
                v = _VDrain(v)
                v.wait_ge(sm("sSM"), M_sq1)
                _newton_rstd(v, st1, 1, 2, 3, 4, 5)
                for t in range(8):
                    v.tensor_scalar(lnbuf[:, t, :], x_s[:, t, :],
                                    st1[:, 0, t:t + 1], st1[:, 5, t:t + 1],
                                    ALU.add, ALU.mult)
                v.engine_nop().then_inc(sm("sVE"), 1)
            M_ln1 = M_var1 + 1

            dump("scl0", gscf[:], "sVE", M_var1)

            @blk.tensor
            def _(t_):
                t_.wait_ge(sm("sVE"), M_ln1)
                t_.wait_ge(sm("sIN"), M_in)
                for i in range(128):
                    st_, ft = divmod(i, 16)
                    if i >= 2:
                        t_.wait_ge(sm("sVE"), M_ln1 + i - 1)
                    t_.transpose(pst[:, i % 2, 0:128],
                                 lnbuf[:, st_, 128 * ft:128 * (ft + 1)],
                                 eye_s[:]).then_inc(sm("sT1"), 1)

            @blk.vector
            def _(v):
                v = _VDrain(v)
                for i in range(128):
                    st_, ft = divmod(i, 16)
                    v.wait_ge(sm("sT1"), i + 1)
                    v.tensor_copy(ln1t[:, ft, 128 * st_:128 * (st_ + 1)],
                                  pst[:, i % 2, 0:128]).then_inc(sm("sVE"), 1)
            M_ln1t = M_ln1 + 128

            @blk.scalar
            def _(sc):
                sc.wait_ge(sm("sVE"), M_ln1t)
                for ft in range(16):
                    if ft == 8:
                        sc.wait_ge(sm("sLNW"), 128)
                    sc.dma_start(
                        out=bass.AP(ln_in, 128 * ft * SC, [[SC, 128], [1, SC]]),
                        in_=ln1t[:, ft, :]).then_inc(sm("sLNW"), 16)
            M_lnw = 16 * 16

            dump("ey0", eye_s[:], "sT1", 128)

        ph1.close()
        stk_qk = ExitStack()
        qk_s = stk_qk.enter_context(nc.sbuf_tensor("qk_s", [128, 4, S], BF16))

        with ExitStack() as phB:
            PB = phB.enter_context
            wqk_s = PB(nc.sbuf_tensor("wqk_s", [128, 16, 4 * D], BF16))
            rhs_b = PB(nc.sbuf_tensor("rhs_b", [128, 2, 16, 512], BF16))
            wv_b = rhs_b
            psv = PB(nc.psum_tensor("psv", [128, 2, 512], F32))
            psq = PB(nc.psum_tensor("psq", [128, 4, 512], F32))

            @blk.gpsimd
            def _(g):
                g.wait_ge(sm("sT1"), 128)
                for oc in range(4):
                    if oc >= 1:
                        g.wait_ge(sm("sTV"), oc * 8)
                        g.wait_ge(sm("sWV"), 256 * oc)
                    for t in range(16):
                        g.dma_start(out=wv_b[:, 0, t, :],
                                    in_=wv_d.ap()[128 * t:128 * (t + 1),
                                                  512 * oc:512 * (oc + 1)]
                                    ).then_inc(sm("sWV"), 16)
                if "ey1" in dbg:
                    g.dma_start(out=dbg["ey1"].ap(), in_=eye_s[:]
                                ).then_inc(sm("sDBG"), 16)
                g.wait_ge(sm("sGIN"), M_gin_packed)
                for t in range(16):
                    g.dma_start(out=wqk_s[:, t, :],
                                in_=wqk_d.ap()[128 * t:128 * (t + 1), :]
                                ).then_inc(sm("sGIN"), 16)
                if "ey2" in dbg:
                    g.dma_start(out=dbg["ey2"].ap(), in_=eye_s[:]
                                ).then_inc(sm("sDBG"), 16)
                g.wait_ge(sm("sLNW"), 128)
                g.collective_compute(
                    "AllGather", ALU.bypass, replica_groups=[list(range(NC))],
                    ins=[ln_in.ap()[0:H // 2, :].opt()],
                    outs=[ln_outA.ap().opt()],
                ).then_inc(sm("sCC"), 1)
                g.wait_ge(sm("sLNW"), M_lnw)
                g.collective_compute(
                    "AllGather", ALU.bypass, replica_groups=[list(range(NC))],
                    ins=[ln_in.ap()[H // 2:H, :].opt()],
                    outs=[ln_outB.ap().opt()],
                ).then_inc(sm("sCC"), 1)
                if "ey3" in dbg:
                    g.wait_ge(sm("sCC"), 2)
                    g.dma_start(out=dbg["ey3"].ap(), in_=eye_s[:]
                                ).then_inc(sm("sDBG"), 16)
            M_cc_ln = 2
            M_gin = M_gin_packed + 16 * 16
            for _n in ("ey1", "ey2", "ey3"):
                if _n in dbg:
                    n_dbg[0] += 16

            @blk.tensor
            def _(t_):
                t_.wait_ge(sm("sVE"), M_ln1t)
                for oc in range(4):
                    t_.wait_ge(sm("sWV"), 256 * (oc + 1))
                    for st_ in range(8):
                        gidx = oc * 8 + st_
                        if gidx >= 2:
                            t_.wait_ge(sm("sEVV"), gidx - 1)
                        for ht in range(16):
                            mm = t_.matmul(psv[:, gidx % 2, :],
                                           ln1t[:, ht, 128 * st_:128 * (st_ + 1)],
                                           wv_b[:, 0, ht, :],
                                           start=(ht == 0), stop=(ht == 15))
                        mm.then_inc(sm("sTV"), 1)

            @blk.vector
            def _(v):
                v = _VDrain(v)
                for gidx in range(32):
                    oc, st_ = divmod(gidx, 8)
                    v.wait_ge(sm("sTV"), gidx + 1)
                    v.tensor_copy(v_s[:, st_, 512 * oc:512 * (oc + 1)],
                                  psv[:, gidx % 2, :]).then_inc(sm("sEVV"), 1)
            M_evv = 32

            @blk.sync
            def _(sy):
                sy.wait_ge(sm("sCC"), 1)
                sy.wait_ge(sm("sTV"), 32)
                for i in range(16):
                    r, scn = divmod(i, 2)
                    if i >= 1:
                        sy.wait_ge(sm("sTQ"), max(0, (i - 1)) * 4)
                        sy.wait_ge(sm("sRHS"), 128 * i)
                    for ht in range(8):
                        sy.dma_start(
                            out=rhs_b[:, i % 2, ht, :],
                            in_=ln_outA.ap()[r][128 * ht:128 * (ht + 1),
                                                512 * scn:512 * (scn + 1)]
                        ).then_inc(sm("sRHS"), 16)

            @blk.scalar
            def _(sc):
                sc.wait_ge(sm("sCC"), 2)
                sc.wait_ge(sm("sTV"), 32)
                for i in range(16):
                    r, scn = divmod(i, 2)
                    if i >= 1:
                        sc.wait_ge(sm("sTQ"), max(0, (i - 1)) * 4)
                        sc.wait_ge(sm("sRHS2"), 128 * i)
                    for ht in range(8, 16):
                        sc.dma_start(
                            out=rhs_b[:, i % 2, ht, :],
                            in_=ln_outB.ap()[r][128 * (ht - 8):128 * (ht - 7),
                                                512 * scn:512 * (scn + 1)]
                        ).then_inc(sm("sRHS2"), 16)

            @blk.tensor
            def _(t_):
                t_.wait_ge(sm("sGIN"), M_gin)
                for i in range(16):
                    t_.wait_ge(sm("sRHS"), 128 * (i + 1))
                    t_.wait_ge(sm("sRHS2"), 128 * (i + 1))
                    if i >= 1:
                        t_.wait_ge(sm("sEVQ"), 4 * i)
                    for ft in range(4):
                        for ht in range(16):
                            mm = t_.matmul(psq[:, ft, :],
                                           wqk_s[:, ht, 128 * ft:128 * (ft + 1)],
                                           rhs_b[:, i % 2, ht, :],
                                           start=(ht == 0), stop=(ht == 15))
                        mm.then_inc(sm("sTQ"), 1)

            @blk.vector
            def _(v):
                v = _VDrain(v)
                for i in range(16):
                    r, scn = divmod(i, 2)
                    off = 1024 * r + 512 * scn
                    for ft in range(4):
                        v.wait_ge(sm("sTQ"), 4 * i + ft + 1)
                        v.tensor_copy(qk_s[:, ft, off:off + 512],
                                      psq[:, ft, :]).then_inc(sm("sEVQ"), 1)
            M_evq = 64

            dump("qk", qk_s[:], "sEVQ", M_evq)
            dump("v", v_s[:], "sEVV", M_evv)
            dump("ey4", eye_s[:], "sEVQ", M_evq)

        with ExitStack() as p5:
            P = p5.enter_context
            e_s = P(nc.sbuf_tensor("e_s", [128, HL, S], BF16))
            q2_s = e_s[:, 0, :]
            negm = P(nc.sbuf_tensor("negm", [128, HL, 64], F32))
            zf = P(nc.sbuf_tensor("zf", [128, HL, 2, 16], F32))
            rb8 = P(nc.sbuf_tensor("rb8", [128, HL, 2, 8], BF16))
            zs = P(nc.sbuf_tensor("zs", [128, HL, 4], F32))
            kmxf = P(nc.sbuf_tensor("kmxf", [128, HL], F32))
            kk_sb = P(nc.sbuf_tensor("kk_sb", [1, 2], F32))
            ones_r = P(nc.sbuf_tensor("ones_r", [1, 128], BF16))
            if "kms" in dbg:
                kms_dbg = P(nc.sbuf_tensor("kms_dbg", [128, 2], F32))
                psc_dbg = P(nc.sbuf_tensor("psc_dbg", [128, 2, 64], F32))
                q2s_dbg = P(nc.sbuf_tensor("q2s_dbg", [128, 64], BF16))
            if "q2t" in dbg:
                q2t_dbg = P(nc.sbuf_tensor("q2t_dbg", [128, 128], BF16))
            if "pst" in dbg:
                pst_dbg = P(nc.sbuf_tensor("pst_dbg", [1, 2, 128], F32))
            kk8 = P(nc.sbuf_tensor("kk8", [1, 8], BF16))
            mh_sc = P(nc.sbuf_tensor("mh_sc", [128, 64], F32))
            c_sb = P(nc.sbuf_tensor("c_sb", [128, HL, 64], F32))
            cbf = P(nc.sbuf_tensor("cbf", [128, HL, 64], BF16))
            c_tb = P(nc.sbuf_tensor("c_tb", [64, HL, 128], BF16))
            dq_mag = P(nc.sbuf_tensor("dq_mag", [128, 1024], I32))
            dq_sg = P(nc.sbuf_tensor("dq_sg", [128, 1024], I32))
            dq_a = P(nc.sbuf_tensor("dq_a", [128, 1024], BF16))
            dq_b = P(nc.sbuf_tensor("dq_b", [128, 1024], BF16))
            dq_cw = P(nc.sbuf_tensor("dq_cw", [128, 1024], BF16))
            dq_w = P(nc.sbuf_tensor("dq_w", [128, 2, 2048], BF16))
            dq_stg = P(nc.sbuf_tensor("dq_stg", [128, 1, 2048], BF16))
            dq_dnb = P(nc.sbuf_tensor("dq_dnb", [128, 4, 128], BF16))
            ps_s = P(nc.psum_tensor("ps_s", [128, HL, 2, 512], F32))
            ps_c = P(nc.psum_tensor("ps_c", [128, HL, 512], F32))
            ps_t = P(nc.psum_tensor("ps_t", [128, 2, 1024], BF16))

            vm = 0
            tm = 0
            smm = M_sq1
            for h in range(HL):
                vm0, tm0 = vm, tm

                @blk.vector
                def _(v, h=h, vm0=vm0):
                    v = _VDrain(v)
                    if h == 0:
                        v.wait_ge(sm("sEVQ"), M_evq)
                        v.memset(rb8[:], 0.0)
                        v.memset(ones_r[:], 1.0)
                    v.tensor_tensor(q2_s[:], qk_s[:, 2 + h, :],
                                    qk_s[:, 2 + h, :], ALU.mult)
                    v.engine_nop().then_inc(sm("sVM"), 1)
                vm += 1

                @blk.tensor
                def _(t_, vm_=vm, h=h):
                    t_.wait_ge(sm("sVM"), vm_)
                    if h == 0:
                        t_.wait_ge(sm("sEVV"), M_evv)
                    for kt in range(64):
                        mm = t_.matmul(ps_c[:, 0, 8 * kt:8 * kt + 8],
                                       q2_s[:, 128 * kt:128 * (kt + 1)],
                                       ones8[:], start=(kt == 0), stop=True,
                                       skip_group_check=True)
                    mm.then_inc(sm("sTM"), 1)
                tm += 1

                @blk.vector
                def _(v, tm_=tm, h=h):
                    v = _VDrain(v)
                    v.wait_ge(sm("sTM"), tm_)
                    v.reduce_max(kmxf[:, h:h + 1], ps_c[:, 0, 0:512:8], axis=AX.X)
                    if "kms" in dbg:
                        v.tensor_copy(kms_dbg[:, h:h + 1], kmxf[:, h:h + 1])
                        v.tensor_copy(psc_dbg[:, h, :], ps_c[:, 0, 0:512:8])
                        if h == 0:
                            v.tensor_copy(q2s_dbg[:], q2_s[:, 0:8192:128])
                    if "q2t" in dbg and h == 0:
                        v.tensor_copy(q2t_dbg[:], q2_s[:, 0:128])
                    v.memset(q2_s[:, 0:128], 1.0)
                    v.tensor_scalar(q2_s[:, 0:128], q2_s[:, 0:128],
                                    kmxf[:, h:h + 1], None, ALU.mult)
                    v.engine_nop().then_inc(sm("sVM"), 1)
                vm += 1

                @blk.tensor
                def _(t_, vm_=vm):
                    t_.wait_ge(sm("sVM"), vm_)
                    t_.transpose(ps_t[:, 0, 0:128], q2_s[:, 0:128], eye_s[:]
                                 ).then_inc(sm("sTM"), 1)
                tm += 1

                @blk.vector
                def _(v, tm_=tm, h=h):
                    v = _VDrain(v)
                    v.wait_ge(sm("sTM"), tm_)
                    v.reduce_max(kk_sb[0:1, h:h + 1], ps_t[0:1, 0, 0:128],
                                 axis=AX.X)
                    v.tensor_scalar(kk8[0:1, :], ones_r[0:1, 0:8],
                                    kk_sb[0:1, h:h + 1], None, ALU.mult)
                    if "pst" in dbg:
                        v.tensor_copy(pst_dbg[0:1, h, :], ps_t[0:1, 0, 0:128])
                    v.engine_nop().then_inc(sm("sVM"), 1)
                vm += 1

                @blk.tensor
                def _(t_, vm_=vm):
                    t_.wait_ge(sm("sVM"), vm_)
                    t_.matmul(ps_c[:, 0, 0:8], ones_r[:], kk8[:],
                              start=True, stop=True).then_inc(sm("sTM"), 1)
                tm += 1

                @blk.vector
                def _(v, tm_=tm, h=h):
                    v = _VDrain(v)
                    v.wait_ge(sm("sTM"), tm_)
                    v.tensor_copy(kmxf[:, h:h + 1], ps_c[:, 0, 0:1])
                    v.engine_nop().then_inc(sm("sVM"), 1)
                vm += 1

            for h in range(HL):
                @blk.vector
                def _(v, h=h):
                    v = _VDrain(v)
                    v.tensor_tensor(q2_s[:], qk_s[:, h, :], qk_s[:, h, :],
                                    ALU.mult)
                    v.engine_nop().then_inc(sm("sVM"), 1)
                vm += 1

                @blk.tensor
                def _(t_, vm_=vm):
                    t_.wait_ge(sm("sVM"), vm_)
                    for kt in range(64):
                        mm = t_.matmul(ps_c[:, 0, 8 * kt:8 * kt + 8],
                                       q2_s[:, 128 * kt:128 * (kt + 1)],
                                       ones8[:], start=(kt == 0), stop=True,
                                       skip_group_check=True)
                    mm.then_inc(sm("sTM"), 1)
                tm += 1

                @blk.vector
                def _(v, tm_=tm, h=h):
                    v = _VDrain(v)
                    v.wait_ge(sm("sTM"), tm_)
                    v.tensor_scalar(mh_sc[:], ps_c[:, 0, 0:512:8],
                                    kmxf[:, h:h + 1], None, ALU.mult)
                    v.engine_nop().then_inc(sm("sVM"), 1)
                vm += 1

                @blk.scalar
                def _(sc, vm_=vm, h=h):
                    sc.wait_ge(sm("sVM"), vm_)
                    sc.activation(negm[:, h, :], mh_sc[:], AF.Sqrt,
                                  scale=MARG / 128.0).then_inc(sm("sSM"), 1)
                smm += 1

                @blk.vector
                def _(v, smm_=smm, h=h):
                    v = _VDrain(v)
                    v.wait_ge(sm("sSM"), smm_)
                    v.tensor_scalar(negm[:, h, :], negm[:, h, :], -1.0, None,
                                    ALU.mult)
                    v.engine_nop().then_inc(sm("sVM"), 1)
                vm += 1
            M_vm_prep = vm

            dump("negm", negm[:], "sVM", M_vm_prep)
            dump("kmx", kmxf[:], "sVM", M_vm_prep)
            dump("kks", kk_sb[:], "sVM", M_vm_prep)
            dump("kk8d", kk8[:], "sVM", M_vm_prep)
            if "kms" in dbg:
                dump("kms", kms_dbg[:], "sVM", M_vm_prep)
                dump("psc", psc_dbg[:], "sVM", M_vm_prep)
                dump("q2s", q2s_dbg[:], "sVM", M_vm_prep)
            if "pst" in dbg:
                dump("pst", pst_dbg[:], "sVM", M_vm_prep)
            if "q2t" in dbg:
                dump("q2t", q2t_dbg[:], "sVM", M_vm_prep)
            dump("on8", ones8[:], "sVM", M_vm_prep)
            dump("eyeE", eye_s[:], "sVM", M_vm_prep)
            dump("eyeL", eye_s[:], "sDQ", NUNITS)
            dump("gupd", gup_s[:], "sVM", M_vm_prep)

            dn_dma_ct = [0]
            dn_ev_ct = [0]

            def dq_vec_unit(v, u):
                par = u % 2
                if u >= 2:
                    v.wait_ge(sm("sDQT"), 16 * (u - 1))
                if u < 11:
                    lt = u
                    srcfull = gup_s[:, lt, :]
                    wn = 256
                    scf = (lambda g, lt=lt: gscf[:, lt, g:g + 1])
                else:
                    ht, w0, wn = DN_CHUNKS[u - 11]
                    srcfull = dnp_s[:, ht, w0:w0 + wn]
                    g0 = (8 * w0) // 128
                    scf = (lambda g, ht=ht, g0=g0:
                           dscf[:, ht, g0 + g:g0 + g + 1])
                for sub in range((8 * wn + 1023) // 1024):
                    wsub = min(128, wn - 128 * sub)
                    n = 8 * wsub
                    src = srcfull[:, 128 * sub:128 * sub + wsub]
                    for j in range(8):
                        v.tensor_scalar(dq_mag[:, j:n:8], src, 4 * j, 7,
                                        ALU.logical_shift_right, ALU.bitwise_and)
                    v.tensor_scalar(dq_sg[:, 0:n], dq_mag[:, 0:n], 1, None,
                                    ALU.logical_shift_right)
                    v.tensor_copy(dq_a[:, 0:n], dq_sg[:, 0:n])
                    v.tensor_scalar(dq_mag[:, 0:n], dq_mag[:, 0:n], 1, None,
                                    ALU.bitwise_and)
                    v.tensor_copy(dq_b[:, 0:n], dq_mag[:, 0:n])
                    v.tensor_scalar(dq_cw[:, 0:n], dq_a[:, 0:n], 0.5, 0.5,
                                    ALU.mult, ALU.subtract)
                    v.tensor_tensor(dq_cw[:, 0:n], dq_cw[:, 0:n], dq_a[:, 0:n],
                                    ALU.mult)
                    v.tensor_scalar(dq_cw[:, 0:n], dq_cw[:, 0:n], 1.0, None,
                                    ALU.add)
                    v.tensor_scalar(dq_a[:, 0:n], dq_b[:, 0:n], 0.5, 1.0,
                                    ALU.mult, ALU.add)
                    v.tensor_tensor(dq_a[:, 0:n], dq_a[:, 0:n], dq_cw[:, 0:n],
                                    ALU.mult)
                    v.tensor_scalar(dq_b[:, 0:n], dq_b[:, 0:n], 0.5, None,
                                    ALU.mult)
                    v.tensor_scalar(dq_sg[:, 0:n], dq_sg[:, 0:n], 1, None,
                                    ALU.min)
                    v.tensor_copy(dq_cw[:, 0:n], dq_sg[:, 0:n])
                    v.tensor_tensor(dq_a[:, 0:n], dq_a[:, 0:n], dq_b[:, 0:n],
                                    ALU.subtract)
                    v.tensor_tensor(dq_a[:, 0:n], dq_a[:, 0:n], dq_cw[:, 0:n],
                                    ALU.mult)
                    v.tensor_tensor(dq_cw[:, 0:n], dq_a[:, 0:n], dq_b[:, 0:n],
                                    ALU.add)
                    for j in range(8):
                        v.tensor_scalar(dq_mag[:, j:n:8], src, 4 * j + 3, 1,
                                        ALU.logical_shift_right, ALU.bitwise_and)
                    v.tensor_copy(dq_a[:, 0:n], dq_mag[:, 0:n])
                    v.tensor_scalar(dq_a[:, 0:n], dq_a[:, 0:n], -2.0, 1.0,
                                    ALU.mult, ALU.add)
                    v.tensor_tensor(dq_cw[:, 0:n], dq_cw[:, 0:n], dq_a[:, 0:n],
                                    ALU.mult)
                    for gl in range((n + 127) // 128):
                        g = 8 * sub + gl
                        o0 = 1024 * sub + 128 * gl
                        o1 = min(1024 * sub + n, o0 + 128)
                        v.tensor_scalar(dq_w[:, par, o0:o1],
                                        dq_cw[:, 128 * gl:128 * gl + (o1 - o0)],
                                        scf(g), None, ALU.mult)
                v.engine_nop().then_inc(sm("sDQ"), 1)

            def dq_tr_unit(t_, u):
                par = u % 2
                nblk = 16 if u < 11 else (8 * DN_CHUNKS[u - 11][2]) // 128
                t_.wait_ge(sm("sDQ"), u + 1)
                for b in range(nblk):
                    gi = 16 * u + b
                    if gi >= 2:
                        t_.wait_ge(sm("sDQE"),
                                   16 * u if b < 2 else 16 * u + b - 1)
                    inc = 1 + (16 - nblk if b == nblk - 1 else 0)
                    t_.transpose(ps_t[:, gi % 2, 0:128],
                                 dq_w[:, par, 128 * b:128 * (b + 1)],
                                 eye_s[:]).then_inc(sm("sDQT"), inc)

            def dq_ev_unit(v, u):
                par = u % 2
                nblk = 16 if u < 11 else (8 * DN_CHUNKS[u - 11][2]) // 128
                for b in range(nblk):
                    gi = 16 * u + b
                    v.wait_ge(sm("sDQT"),
                              gi + 1 if b < nblk - 1 else 16 * (u + 1))
                    inc = 1 + (16 - nblk if b == nblk - 1 else 0)
                    if u < 11:
                        if u >= 1 and b == 0:
                            v.wait_ge(sm("sDQW"), 16 * u)
                        v.tensor_copy(dq_stg[:, 0, b:2048:16],
                                      ps_t[:, gi % 2, 0:128]
                                      ).then_inc(sm("sDQE"), inc)
                    else:
                        nb = dn_ev_ct[0]
                        if nb >= 4:
                            v.wait_ge(sm("sDQW2"), 16 * (nb - 3))
                        v.tensor_copy(dq_dnb[:, nb % 4, :],
                                      ps_t[:, gi % 2, 0:128]
                                      ).then_inc(sm("sDQE"), inc)
                        dn_ev_ct[0] += 1

            def dq_dma_unit(sy, u):
                par = u % 2
                if u < 11:
                    sy.wait_ge(sm("sDQE"), 16 * u + 16)
                    if u >= 1:
                        sy.wait_ge(sm("sDQW"), 16 * u)
                    sy.dma_start(
                        out=bass.AP(gu_in, u * 2048, [[11 * 2048, 128], [1, 2048]]),
                        in_=dq_stg[:, 0, :]).then_inc(sm("sDQW"), 16)
                else:
                    ht, w0, wn = DN_CHUNKS[u - 11]
                    nblk = (8 * wn) // 128
                    it0 = (8 * w0) // 128
                    ev_base = 16 * u
                    for b in range(nblk):
                        sy.wait_ge(sm("sDQE"),
                                   ev_base + b + 1 if b < nblk - 1
                                   else 16 * (u + 1))
                        if dn_dma_ct[0] >= 1:
                            sy.wait_ge(sm("sDQW2"), 16 * dn_dma_ct[0])
                        it = it0 + b
                        sy.dma_start(
                            out=bass.AP(dn_in, it * 128 * 256 + 128 * ht,
                                        [[256, 128], [1, 128]]),
                            in_=dq_dnb[:, dn_dma_ct[0] % 4, :]
                        ).then_inc(sm("sDQW2"), 16)
                        dn_dma_ct[0] += 1

            M_dqw = 16 * 11
            M_dqw2 = 16 * 88

            inj_v, inj_t, inj_e, inj_d = {}, {}, {}, {}
            for u in range(NUNITS):
                it0 = 2 + u * 3
                inj_v[it0] = u
                inj_t[it0 + 1] = u
                inj_e[it0 + 1] = u
                inj_d[it0 + 2] = u

            sSCn, sEXn, sRn, sCMn = ["sSC0", "sSC1"], ["sEX0", "sEX1"], \
                ["sR0", "sR1"], ["sCM0", "sCM1"]

            @blk.tensor
            def _(t_):
                t_.wait_ge(sm("sVM"), M_vm_prep)
                for t in range(64):
                    for h in range(HL):
                        if t > 0:
                            t_.wait_ge(sm(sRn[h]), t)
                            for kt in range(64):
                                mm = t_.matmul(ps_c[:, h, 8 * kt:8 * kt + 8],
                                               e_s[:, h, 128 * kt:128 * (kt + 1)],
                                               rb8[:, h, (t - 1) % 2, :],
                                               start=(t == 1 and kt == 0),
                                               stop=True,
                                               skip_group_check=True)
                            mm.then_inc(sm(sCMn[h]), 1)
                        for ch in range(16):
                            g = 16 * t + ch
                            if g >= 2:
                                t_.wait_ge(sm(sEXn[h]), g - 1)
                            t_.matmul(ps_s[:, h, ch % 2, :],
                                      qk_s[:, h, 128 * t:128 * (t + 1)],
                                      qk_s[:, 2 + h, 512 * ch:512 * (ch + 1)],
                                      start=True, stop=True
                                      ).then_inc(sm(sSCn[h]), 1)
                    if t in inj_t:
                        dq_tr_unit(t_, inj_t[t])
                for h in range(HL):
                    t_.wait_ge(sm(sRn[h]), 64)
                    for kt in range(64):
                        mm = t_.matmul(ps_c[:, h, 8 * kt:8 * kt + 8],
                                       e_s[:, h, 128 * kt:128 * (kt + 1)],
                                       rb8[:, h, 1, :],
                                       start=False, stop=True,
                                       skip_group_check=True)
                    mm.then_inc(sm(sCMn[h]), 1)
            M_cm = 64

            @blk.scalar
            def _(sc):
                for t in range(64):
                    for h in range(HL):
                        for ch in range(16):
                            g = 16 * t + ch
                            sc.wait_ge(sm(sSCn[h]), g + 1)
                            if t > 0 and ch == 0:
                                sc.wait_ge(sm(sCMn[h]), t)
                            sc.activation(e_s[:, h, 512 * ch:512 * (ch + 1)],
                                          ps_s[:, h, ch % 2, :], AF.Exp,
                                          bias=negm[:, h, t:t + 1], scale=RSCL,
                                          accum_out=zf[:, h, t % 2, ch:ch + 1]
                                          ).then_inc(sm(sEXn[h]), 1)

            @blk.vector
            def _(v):
                v = _VDrain(v)
                for t in range(64):
                    for h in range(HL):
                        v.wait_ge(sm(sEXn[h]), 16 * (t + 1))
                        if t >= 2:
                            v.wait_ge(sm(sCMn[h]), t - 1)
                        v.reduce_sum(zs[:, h, 0:1], zf[:, h, t % 2, :], axis=AX.X)
                        v.reciprocal(zs[:, h, 1:2], zs[:, h, 0:1])
                        v.tensor_copy(rb8[:, h, t % 2, 0:1], zs[:, h, 1:2]
                                      ).then_inc(sm(sRn[h]), 1)
                    if t in inj_v:
                        dq_vec_unit(v, inj_v[t])
                    if t in inj_e:
                        dq_ev_unit(v, inj_e[t])

            @blk.sync
            def _(sy):
                for t in range(64):
                    if t in inj_d:
                        dq_dma_unit(sy, inj_d[t])

            @blk.vector
            def _(v):
                v = _VDrain(v)
                for h in range(HL):
                    v.wait_ge(sm(sCMn[h]), M_cm)
                    v.tensor_copy(c_sb[:, h, :], ps_c[:, h, 0:512:8])
                v.tensor_copy(cbf[:], c_sb[:])
                v.engine_nop().then_inc(sm("sVA"), 1)

            @blk.tensor
            def _(t_, tm_=tm):
                t_.wait_ge(sm("sVA"), 1)
                t_.transpose(ps_t[0:64, 0, 0:128], cbf[:, 0, :], eye_s[:]
                             ).then_inc(sm("sTM"), 1)
                t_.transpose(ps_t[0:64, 1, 0:128], cbf[:, 1, :], eye_s[:]
                             ).then_inc(sm("sTM"), 1)
            tm += 2

            @blk.vector
            def _(v, tm_=tm):
                v = _VDrain(v)
                v.wait_ge(sm("sTM"), tm_)
                v.tensor_copy(c_tb[:, 0, :], ps_t[0:64, 0, 0:128])
                v.tensor_copy(c_tb[:, 1, :], ps_t[0:64, 1, 0:128])
                v.engine_nop().then_inc(sm("sVA"), 1)
            M_va_c = 2

            @blk.scalar
            def _(sc):
                sc.wait_ge(sm("sVA"), M_va_c)
                sc.dma_start(
                    out=bass.AP(c_in, 0, [[128, 64], [8192, HL], [1, 128]]),
                    in_=c_tb[:]).then_inc(sm("sCW"), 16)
            M_cw = 16

            @blk.gpsimd
            def _(g):
                g.wait_ge(sm("sDQW"), M_dqw)
                g.collective_compute(
                    "AllGather", ALU.bypass, replica_groups=[list(range(NC))],
                    ins=[gu_in.ap().opt()], outs=[gu_out.ap().opt()],
                ).then_inc(sm("sCC"), 1)
                g.wait_ge(sm("sDQW2"), M_dqw2)
                g.collective_compute(
                    "AllGather", ALU.bypass, replica_groups=[list(range(NC))],
                    ins=[dn_in.ap().opt()], outs=[dn_out.ap().opt()],
                ).then_inc(sm("sCC"), 1)
                g.wait_ge(sm("sCW"), M_cw)
                g.collective_compute(
                    "AllGather", ALU.bypass, replica_groups=[list(range(NC))],
                    ins=[c_in.ap().opt()], outs=[c_out.ap().opt()],
                ).then_inc(sm("sCC"), 1)
            M_cc_gu, M_cc_dn, M_cc_c = 3, 4, 5

            dump("wgu", gu_out.ap(), "sCC", M_cc_gu)
            dump("wdn", dn_out.ap(), "sCC", M_cc_dn)
            dump("scl1", gscf[:], "sDQ", NUNITS)
            dump("dsc1", dscf[:], "sDQ", NUNITS)
            dump("cga", c_out.ap(), "sCC", M_cc_c)
        stk_qk.close()

        stk_ln2t = ExitStack()
        with ExitStack() as p6:
            P = p6.enter_context
            ca_bf = P(nc.sbuf_tensor("ca_bf", [16, SC], BF16))
            ct_s = P(nc.sbuf_tensor("ct_s", [128, 8, 16], F32))
            attt = P(nc.sbuf_tensor("attt", [128, 16, SC], BF16))
            xr_s = P(nc.sbuf_tensor("xr_s", [128, 8, H], F32))
            wo_b = P(nc.sbuf_tensor("wo_b", [128, 1, 16, 512], BF16))
            ps6 = P(nc.psum_tensor("ps6", [128, 2, 512], F32))
            ps6b = P(nc.psum_tensor("ps6b", [128, 2, 1024], BF16))

            TA_ct0, TA_attt0, TA_wo0, TA_ln2t0 = 0, 8, 136, 168
            VA_base = M_va_c
            VA_ct0 = VA_base
            VA_att = VA_ct0 + 8 + 1
            VA_attt0 = VA_att
            VA_x20 = VA_attt0 + 128
            VA_var2 = VA_x20 + 32 + 1
            VA_ln2 = VA_var2 + 1
            VA_ln2t0 = VA_ln2
            VA_ln2t = VA_ln2t0 + 128

            @blk.sync
            def _(sy):
                sy.wait_ge(sm("sCM0"), 64)
                sy.wait_ge(sm("sCM1"), 64)
                sy.wait_ge(sm("sCW"), M_cw)
                if n_dbg[0]:
                    sy.wait_ge(sm("sDBG"), n_dbg[0])
                for t in range(8):
                    sy.dma_start(out=xr_s[:, t, :],
                                 in_=x_d.ap()[128 * t:128 * (t + 1), :]
                                 ).then_inc(sm("sXR"), 16)
            M_xr = 128

            @blk.gpsimd
            def _(g):
                g.wait_ge(sm("sCC"), M_cc_c)
                if n_dbg[0]:
                    g.wait_ge(sm("sDBG"), n_dbg[0])
                with g.register("coff") as coff:
                    g.reg_load(coff, meta_s[0:1, 0:1])
                    g.dma_start(out=ca_bf[:],
                                in_=bass.AP(c_out, coff, [[S, 16], [1, SC]])
                                ).then_inc(sm("sCA"), 16)

            @blk.tensor
            def _(t_):
                t_.wait_ge(sm("sCA"), 16)
                for st_ in range(8):
                    if st_ >= 2:
                        t_.wait_ge(sm("sVA"), VA_ct0 + st_ - 1)
                    t_.transpose(ps6b[:, st_ % 2, 0:16],
                                 ca_bf[:, 128 * st_:128 * (st_ + 1)],
                                 eye_s[0:16, 0:16]).then_inc(sm("sTA"), 1)

            @blk.vector
            def _(v):
                v = _VDrain(v)
                for st_ in range(8):
                    v.wait_ge(sm("sTA"), TA_ct0 + st_ + 1)
                    v.tensor_copy(ct_s[:, st_, :], ps6b[:, st_ % 2, 0:16]
                                  ).then_inc(sm("sVA"), 1)
                for st_ in range(8):
                    for hh in range(16):
                        v.tensor_scalar(v_s[:, st_, 128 * hh:128 * (hh + 1)],
                                        v_s[:, st_, 128 * hh:128 * (hh + 1)],
                                        ct_s[:, st_, hh:hh + 1], None, ALU.mult)
                v.engine_nop().then_inc(sm("sVA"), 1)

            @blk.tensor
            def _(t_):
                t_.wait_ge(sm("sVA"), VA_att)
                for i in range(128):
                    st_, ft = divmod(i, 16)
                    if i >= 2:
                        t_.wait_ge(sm("sVA"), VA_attt0 + i - 1)
                    t_.transpose(ps6b[:, i % 2, 0:128],
                                 v_s[:, st_, 128 * ft:128 * (ft + 1)],
                                 eye_s[:]).then_inc(sm("sTA"), 1)

            @blk.vector
            def _(v):
                v = _VDrain(v)
                for i in range(128):
                    st_, ft = divmod(i, 16)
                    v.wait_ge(sm("sTA"), TA_attt0 + i + 1)
                    v.tensor_copy(attt[:, ft, 128 * st_:128 * (st_ + 1)],
                                  ps6b[:, i % 2, 0:128]).then_inc(sm("sVA"), 1)
            lnb2 = v_s
            ln2t = ln1t
            bns2 = P(nc.sbuf_tensor("bns2", [128, 8, 4, 6], F32))
            mv2 = P(nc.sbuf_tensor("mv2", [128, 8, 2], F32))
            st2 = P(nc.sbuf_tensor("st2", [128, 6, 8], F32))

            @blk.gpsimd
            def _(g):
                g.wait_ge(sm("sCM0"), 64)
                g.wait_ge(sm("sCM1"), 64)
                for oc in range(4):
                    if oc >= 1:
                        g.wait_ge(sm("sTA"), TA_wo0 + 8 * oc)
                        g.wait_ge(sm("sWO"), 256 * oc)
                    for t in range(16):
                        g.dma_start(out=wo_b[:, 0, t, :],
                                    in_=wo_d.ap()[128 * t:128 * (t + 1),
                                                  512 * oc:512 * (oc + 1)]
                                    ).then_inc(sm("sWO"), 16)

            @blk.tensor
            def _(t_):
                t_.wait_ge(sm("sVA"), VA_attt0 + 128)
                for oc in range(4):
                    t_.wait_ge(sm("sWO"), 256 * (oc + 1))
                    for st_ in range(8):
                        gidx = oc * 8 + st_
                        if gidx >= 2:
                            t_.wait_ge(sm("sVA"), VA_x20 + gidx - 1)
                        for ft in range(16):
                            mm = t_.matmul(ps6[:, gidx % 2, :],
                                           attt[:, ft, 128 * st_:128 * (st_ + 1)],
                                           wo_b[:, 0, ft, :],
                                           start=(ft == 0), stop=(ft == 15))
                        mm.then_inc(sm("sTA"), 1)

            @blk.vector
            def _(v):
                v = _VDrain(v)
                v.wait_ge(sm("sXR"), M_xr)
                for gidx in range(32):
                    oc, st_ = divmod(gidx, 8)
                    v.wait_ge(sm("sTA"), TA_wo0 + gidx + 1)
                    v.tensor_tensor(xr_s[:, st_, 512 * oc:512 * (oc + 1)],
                                    ps6[:, gidx % 2, :],
                                    xr_s[:, st_, 512 * oc:512 * (oc + 1)],
                                    ALU.add).then_inc(sm("sVA"), 1)

            @blk.vector
            def _(v):
                v = _VDrain(v)
                v.wait_ge(sm("sVA"), VA_x20 + 32)
                for t in range(8):
                    for k in range(4):
                        v.bn_stats(bns2[:, t, k, :],
                                   xr_s[:, t, 512 * k:512 * (k + 1)])
                    v.bn_aggr(mv2[:, t, :], bns2[:, t, :, :])
                v.tensor_scalar(st2[:, 0, :], mv2[:, :, 0], -1.0, None, ALU.mult)
                v.tensor_scalar(st2[:, 1, :], mv2[:, :, 1], LN_EPS, None, ALU.add)
                v.engine_nop().then_inc(sm("sVA"), 1)

            @blk.scalar
            def _(sc):
                sc.wait_ge(sm("sVA"), VA_var2)
                sc.activation(st2[:, 2, :], st2[:, 1, :], AF.Sqrt
                              ).then_inc(sm("sSA"), 1)
            M_sa_sq2 = 1

            @blk.vector
            def _(v):
                v = _VDrain(v)
                v.wait_ge(sm("sSA"), M_sa_sq2)
                _newton_rstd(v, st2, 1, 2, 3, 4, 5)
                for t in range(8):
                    v.tensor_scalar(lnb2[:, t, :], xr_s[:, t, :],
                                    st2[:, 0, t:t + 1], st2[:, 5, t:t + 1],
                                    ALU.add, ALU.mult)
                v.engine_nop().then_inc(sm("sVA"), 1)

            @blk.tensor
            def _(t_):
                t_.wait_ge(sm("sVA"), VA_ln2)
                for i in range(128):
                    st_, ft = divmod(i, 16)
                    if i >= 2:
                        t_.wait_ge(sm("sVA"), VA_ln2t0 + i - 1)
                    t_.transpose(ps6b[:, i % 2, 0:128],
                                 lnb2[:, st_, 128 * ft:128 * (ft + 1)],
                                 eye_s[:]).then_inc(sm("sTA"), 1)

            @blk.vector
            def _(v):
                v = _VDrain(v)
                for i in range(128):
                    st_, ft = divmod(i, 16)
                    v.wait_ge(sm("sTA"), TA_ln2t0 + i + 1)
                    v.tensor_copy(ln2t[:, ft, 128 * st_:128 * (st_ + 1)],
                                  ps6b[:, i % 2, 0:128]).then_inc(sm("sVA"), 1)

            @blk.sync
            def _(sy):
                sy.wait_ge(sm("sVA"), VA_ln2t)
                for t in range(8):
                    sy.dma_start(out=x2_h.ap()[128 * t:128 * (t + 1), :],
                                 in_=xr_s[:, t, :]).then_inc(sm("sX2S"), 16)
            M_x2s = 128

            dump("x2", x2_h.ap(), "sX2S", M_x2s)

        with ExitStack() as p8:
            P = p8.enter_context
            hact = P(nc.sbuf_tensor("hact", [128, 44, SC], BF16))
            ps_gu = P(nc.psum_tensor("ps_gu", [128, 8, 512], F32))
            stk_gu = ExitStack()
            guw = stk_gu.enter_context(nc.sbuf_tensor("guw", [128, 4, 2048], BF16))
            tmpg = stk_gu.enter_context(nc.sbuf_tensor("tmpg", [128, 2, 2, 512], BF16))

            TA_gu0 = 296
            TA_dn0 = TA_gu0 + 176
            VA_h0 = VA_ln2t

            def gu_src(ot):
                r, lt = divmod(ot, 11)
                return bass.AP(gu_out, r * (128 * 11 * 2048) + lt * 2048,
                               [[11 * 2048, 128], [1, 2048]])

            @blk.gpsimd
            def _(g):
                g.wait_ge(sm("sCC"), M_cc_gu)
                g.wait_ge(sm("sVA"), VA_ln2t)
                g.wait_ge(sm("sX2S"), M_x2s)
                for j in range(44):
                    if j >= 1:
                        g.wait_ge(sm("sTA"), TA_gu0 + max(0, 4 * (j - 1)))
                        g.wait_ge(sm("sGWg"), 16 * j)
                    g.dma_start(out=guw[:, (2 * j) % 4, :], in_=gu_src(j)
                                ).then_inc(sm("sGWg"), 16)

            @blk.sync
            def _(sy):
                sy.wait_ge(sm("sCC"), M_cc_gu)
                sy.wait_ge(sm("sVA"), VA_ln2t)
                sy.wait_ge(sm("sX2S"), M_x2s)
                for j in range(44):
                    if j >= 1:
                        sy.wait_ge(sm("sTA"), TA_gu0 + max(0, 4 * (j - 1)))
                        sy.wait_ge(sm("sGWu"), 16 * j)
                    sy.dma_start(out=guw[:, (2 * j + 1) % 4, :], in_=gu_src(44 + j)
                                 ).then_inc(sm("sGWu"), 16)

            @blk.tensor
            def _(t_):
                t_.wait_ge(sm("sVA"), VA_ln2t)
                for j in range(44):
                    if j >= 2:
                        t_.wait_ge(sm("sVA"), VA_h0 + 2 * (j - 1))
                    t_.wait_ge(sm("sGWg"), 16 * (j + 1))
                    t_.wait_ge(sm("sGWu"), 16 * (j + 1))
                    for half in range(2):
                        for scn in range(2):
                            bank = 4 * (j % 2) + 2 * half + scn
                            for ht in range(16):
                                mm = t_.matmul(ps_gu[:, bank, :],
                                               guw[:, (2 * j + half) % 4, ht:2048:16],
                                               ln2t[:, ht, 512 * scn:512 * (scn + 1)],
                                               start=(ht == 0), stop=(ht == 15))
                            mm.then_inc(sm("sTA"), 1)

            @blk.scalar
            def _(sc):
                for j in range(44):
                    for scn in range(2):
                        sc.wait_ge(sm("sTA"), TA_gu0 + 4 * j + scn + 1)
                        if j >= 2:
                            sc.wait_ge(sm("sVA"), VA_h0 + 2 * (j - 1))
                        sc.activation(tmpg[:, j % 2, scn, :],
                                      ps_gu[:, 4 * (j % 2) + scn, :], AF.Silu
                                      ).then_inc(sm("sSA"), 1)
            SA_gu0 = M_sa_sq2

            @blk.vector
            def _(v):
                v = _VDrain(v)
                for j in range(44):
                    for scn in range(2):
                        v.wait_ge(sm("sSA"), SA_gu0 + 2 * j + scn + 1)
                        v.wait_ge(sm("sTA"), TA_gu0 + 4 * j + 2 + scn + 1)
                        v.tensor_tensor(hact[:, j, 512 * scn:512 * (scn + 1)],
                                        tmpg[:, j % 2, scn, :],
                                        ps_gu[:, 4 * (j % 2) + 2 + scn, :],
                                        ALU.mult).then_inc(sm("sVA"), 1)
            VA_hact = VA_h0 + 88

            dump("hact", hact[:], "sVA", VA_hact)

            stk_gu.close()
            dnw2 = P(nc.sbuf_tensor("dnw2", [128, 12, 512], BF16))
            lnt_flat = ln1t[:].rearrange("p a b -> p (a b)")
            y_bf = v_s

            def dnw_ap(it, c0=0, c1=512):
                if it < 32:
                    return lnt_flat[:, 512 * it + c0:512 * it + c1]
                return dnw2[:, it - 32, c0:c1]
            x2r = P(nc.sbuf_tensor("x2r", [128, 2, H], F32))
            bns3 = P(nc.sbuf_tensor("bns3", [128, 8, 4, 6], F32))
            mv3 = P(nc.sbuf_tensor("mv3", [128, 8, 2], F32))
            st3 = P(nc.sbuf_tensor("st3", [128, 6, 8], F32))

            @blk.sync
            def _(sy):
                sy.wait_ge(sm("sCC"), M_cc_dn)
                sy.wait_ge(sm("sVA"), VA_hact)
                npair = 0
                for hc in range(4):
                    for it in range(44):
                        if hc >= 1:
                            sy.wait_ge(sm("sTA"), TA_dn0 + 44 * (hc - 1) + it + 1)
                        if npair % 4 == 0 and npair >= 4:
                            sy.wait_ge(sm("sDW"), 32 * npair)
                        for rr in range(2):
                            sy.dma_start(
                                out=dnw_ap(it, 256 * rr, 256 * (rr + 1)),
                                in_=dn_out.ap()[2 * hc + rr][it]
                            ).then_inc(sm("sDW"), 16)
                        npair += 1

            @blk.tensor
            def _(t_):
                t_.wait_ge(sm("sVA"), VA_hact)
                for hc in range(4):
                    if hc >= 1:
                        t_.wait_ge(sm("sYEV"), 8 * hc)
                    for it in range(44):
                        p = 44 * hc + it
                        t_.wait_ge(sm("sDW"), min(128 * (p // 4 + 1), 32 * 176))
                        for st_ in range(8):
                            mm = t_.matmul(ps_gu[:, st_, :],
                                           hact[:, it, 128 * st_:128 * (st_ + 1)],
                                           dnw_ap(it),
                                           start=(it == 0), stop=(it == 43),
                                           skip_group_check=True)
                        mm.then_inc(sm("sTA"), 1)

            @blk.vector
            def _(v):
                v = _VDrain(v)
                for hc in range(4):
                    v.wait_ge(sm("sTA"), TA_dn0 + 44 * (hc + 1))
                    for st_ in range(8):
                        v.tensor_copy(y_bf[:, st_, 512 * hc:512 * (hc + 1)],
                                      ps_gu[:, st_, :]).then_inc(sm("sYEV"), 1)
            M_yev = 32

            @blk.vector
            def _(v):
                v = _VDrain(v)
                v.wait_ge(sm("sYEV"), M_yev)
                for t in range(8):
                    for k in range(4):
                        v.bn_stats(bns3[:, t, k, :],
                                   y_bf[:, t, 512 * k:512 * (k + 1)])
                    v.bn_aggr(mv3[:, t, :], bns3[:, t, :, :])
                v.tensor_tensor(st3[:, 0, :], mv3[:, :, 0], mv3[:, :, 0],
                                ALU.mult)
                v.tensor_tensor(st3[:, 1, :], st3[:, 0, :], mv3[:, :, 1],
                                ALU.add)
                v.tensor_scalar(st3[:, 1, :], st3[:, 1, :], RMS_EPS, None,
                                ALU.add)
                v.engine_nop().then_inc(sm("sVA"), 1)
            VA_rms = VA_hact + 1

            @blk.scalar
            def _(sc):
                sc.wait_ge(sm("sVA"), VA_rms)
                sc.activation(st3[:, 2, :], st3[:, 1, :], AF.Sqrt
                              ).then_inc(sm("sSA"), 1)
            SA_rms = SA_gu0 + 88 + 1

            @blk.vector
            def _(v):
                v = _VDrain(v)
                v.wait_ge(sm("sSA"), SA_rms)
                _newton_rstd(v, st3, 1, 2, 3, 4, 5)
                v.engine_nop().then_inc(sm("sVA"), 1)
            VA_rstd = VA_rms + 1

            @blk.sync
            def _(sy):
                sy.wait_ge(sm("sX2S"), M_x2s)
                for t in range(8):
                    if t >= 1:
                        sy.wait_ge(sm("sVA"), VA_rstd + max(0, t - 1))
                        sy.wait_ge(sm("sX2R"), 16 * t)
                        sy.wait_ge(sm("sOUT"), 16 * max(0, t - 1))
                    sy.dma_start(out=x2r[:, t % 2, :],
                                 in_=x2_h.ap()[128 * t:128 * (t + 1), :]
                                 ).then_inc(sm("sX2R"), 16)

            @blk.vector
            def _(v):
                v = _VDrain(v)
                v.wait_ge(sm("sVA"), VA_rstd)
                for t in range(8):
                    v.wait_ge(sm("sX2R"), 16 * max(t + 1, 2))
                    if t >= 2:
                        v.wait_ge(sm("sOUT"), 16 * (t - 1))
                    v.scalar_tensor_tensor(x2r[:, t % 2, :], y_bf[:, t, :],
                                           st3[:, 5, t:t + 1], x2r[:, t % 2, :],
                                           ALU.mult, ALU.add
                                           ).then_inc(sm("sVA"), 1)
            VA_out0 = VA_rstd

            @blk.scalar
            def _(sc):
                for t in range(8):
                    sc.wait_ge(sm("sVA"), VA_out0 + t + 1)
                    if t >= 1:
                        sc.wait_ge(sm("sOUT"), 16 * t)
                    sc.dma_start(out=y_d.ap()[128 * t:128 * (t + 1), :],
                                 in_=x2r[:, t % 2, :]).then_inc(sm("sOUT"), 16)

            @blk.sync
            def _(sy):
                sy.wait_ge(sm("sOUT"), 128)
                if n_dbg[0]:
                    sy.wait_ge(sm("sDBG"), n_dbg[0])

    return nc, dbg


def _host_inputs(x_in, w_qkv_in, w_o_in, gup_in, gsc_in, dnp_in, dsc_in):
    x = np.ascontiguousarray(np.asarray(x_in, dtype=np.float32).reshape(S, H))
    wqkv = np.asarray(w_qkv_in, dtype=np.float32)
    wo = np.ascontiguousarray(np.asarray(w_o_in, dtype=np.float32))
    gup = np.asarray(gup_in, dtype=np.int32)
    gsc = np.asarray(gsc_in, dtype=np.float16)
    dnp = np.asarray(dnp_in, dtype=np.int32)
    dsc = np.asarray(dsc_in, dtype=np.float16)
    wv = np.ascontiguousarray(wqkv[:, 2 * H:3 * H])
    eye = np.eye(128, dtype=ml_dtypes.bfloat16)
    in_maps = []
    for c in range(NC):
        cols = np.r_[256 * c:256 * c + 256, H + 256 * c:H + 256 * c + 256]
        in_maps.append(dict(
            x=np.ascontiguousarray(x[SC * c:SC * (c + 1)]),
            wqk=np.ascontiguousarray(wqkv[:, cols]),
            wv=wv,
            wo=wo,
            gup=np.ascontiguousarray(gup[GU_SH * c:GU_SH * (c + 1)]),
            gsc=np.ascontiguousarray(gsc[:, GU_SH * c:GU_SH * (c + 1)].T),
            dnp=np.ascontiguousarray(dnp[DN_SH * c:DN_SH * (c + 1)]),
            dsc=np.ascontiguousarray(dsc[:, DN_SH * c:DN_SH * (c + 1)].T),
            eye=eye,
            meta=np.array([[SC * c, c]], dtype=np.int32),
        ))
    return in_maps


def kernel(L0_attn_x, L0_attn_w_qkv, L0_attn_w_o, L0_ffn_gate_up_packed,
           L0_ffn_gate_up_scales, L0_ffn_down_packed, L0_ffn_down_scales):
    in_maps = _host_inputs(L0_attn_x, L0_attn_w_qkv, L0_attn_w_o,
                           L0_ffn_gate_up_packed, L0_ffn_gate_up_scales,
                           L0_ffn_down_packed, L0_ffn_down_scales)
    nc, dbg = build_nc(_DEBUG)
    res = run_bass_kernel_spmd(nc, in_maps, core_ids=list(range(NC)))
    if _DEBUG:
        kernel._last = res
    out = np.concatenate([res.results[c]["y"] for c in range(NC)], axis=0)
    return out.reshape(1, S, H).astype(np.float32)
